# version 09
# baseline (speedup 1.0000x reference)
import numpy as np
import ml_dtypes
from contextlib import ExitStack
import concourse.bass as bass
import concourse.mybir as mybir
from concourse.bass_utils import run_bass_kernel_spmd

F32 = mybir.dt.float32
BF16 = mybir.dt.bfloat16
U32 = mybir.dt.uint32
AF = mybir.ActivationFunctionType
ALU = mybir.AluOpType

B, T, D = 2, 8192, 1024
NCORE = 8
NH_A, NH_B, HS = 16, 8, 64
PAD = 768
TP = PAD + T
QMT = 256
NQMT = 8
SHIFT = 20.0
EPS = 1e-5
SEL_K = 1024
BITERS = 46
VW = 16 * 96
SHUF = [(i + 16) % 32 for i in range(32)]
ABL_NOGATE = False
ABL_NOMOSA = False
ABL_NOVM = False
ABL_DUMPDEN = True


def _sigma(ci, j):
    return (ci + j) % 4


def _own_cols(ci):
    idx = []
    for j in range(NQMT):
        base = 1024 * j + 256 * _sigma(ci, j)
        idx.extend(range(base, base + 256))
    return np.asarray(idx, dtype=np.int64)


class PB:
    ENGS = ("sp", "act", "dve", "pool", "pe")

    def __init__(self, nc):
        self.nc = nc
        self.es = ExitStack()
        self.C = {}
        self.S = {}
        for name in ("ld", "st", "gd", "cc", "pe", "ex", "dv", "ms", "av"):
            self.S[name] = self.es.enter_context(nc.semaphore("s_" + name))
            self.C[name] = 0
        self.ops = None

    def sb(self, nm, shp, dt):
        return self.es.enter_context(self.nc.sbuf_tensor(nm, shp, dt))

    def begin(self):
        assert self.ops is None
        self.ops = {e: [] for e in self.ENGS}

    def flush(self):
        ops, self.ops = self.ops, None
        nc = self.nc
        with nc.Block() as blk:
            if ops["sp"]:
                blk.sync(lambda e, _o=ops["sp"]: [f(e) for f in _o])
            if ops["pe"]:
                blk.tensor(lambda e, _o=ops["pe"]: [f(e) for f in _o])
            if ops["act"]:
                blk.scalar(lambda e, _o=ops["act"]: [f(e) for f in _o])
            if ops["dve"]:
                blk.vector(lambda e, _o=ops["dve"]: [f(e) for f in _o])
            if ops["pool"]:
                blk.gpsimd(lambda e, _o=ops["pool"]: [f(e) for f in _o])

    def on(self, eng, fn, sem=None, n=1):
        if sem is not None:
            self.C[sem] += n
            s, k = self.S[sem], n

            def wrapped(e, _fn=fn, _s=s, _k=k):
                ins = _fn(e)
                ins.then_inc(_s, _k)
            self.ops[eng].append(wrapped)
        else:
            self.ops[eng].append(fn)

    def wait(self, eng, sem, val=None):
        v = self.C[sem] if val is None else val
        if v <= 0:
            return
        s = self.S[sem]
        self.ops[eng].append(lambda e, _s=s, _v=v: e.wait_ge(_s, _v))

    def cnt(self, sem):
        return self.C[sem]



_ENG_PREFIX = {"Pool": mybir.EngineType.Pool, "SP": mybir.EngineType.SP,
               "Activation": mybir.EngineType.Activation}


def _free_dyn_regs(nc, nb):
    regs = set()
    for nm in nc.inst_names()[nb:]:
        i2 = nc.find_inst_by_name(nm)
        if i2 is None:
            continue
        ra = i2.regs_accessed() if callable(i2.regs_accessed) else i2.regs_accessed
        for tup in ra:
            for rn in (tup if isinstance(tup, tuple) else (tup,)):
                if '_tmp_' in rn or '_snap_' in rn:
                    regs.add(rn)
    for rn in regs:
        eng = _ENG_PREFIX[rn.split('_', 1)[0]]
        nc.free_register(bass.RegisterHandle(rn, eng))


def build_program():
    nc = bass.Bass("TRN2", target_bir_lowering=False)
    p = PB(nc)

    x_in = nc.dram_tensor("x", [D, 2048], F32, kind="ExternalInput").ap()
    w_q_a = nc.dram_tensor("w_q_a", [D, D], F32, kind="ExternalInput").ap()
    w_k_a = nc.dram_tensor("w_k_a", [D, D], F32, kind="ExternalInput").ap()
    w_v_a = nc.dram_tensor("w_v_a", [D, D], F32, kind="ExternalInput").ap()
    w_o_a = nc.dram_tensor("w_o_a", [D, D], F32, kind="ExternalInput").ap()
    w_q_b = nc.dram_tensor("w_q_b", [D, D], F32, kind="ExternalInput").ap()
    w_k_b = nc.dram_tensor("w_k_b", [D, D], F32, kind="ExternalInput").ap()
    w_v_b = nc.dram_tensor("w_v_b", [D, D], F32, kind="ExternalInput").ap()
    w_o_b = nc.dram_tensor("w_o_b", [D, D], F32, kind="ExternalInput").ap()
    w_rt = nc.dram_tensor("w_rt", [D, 8], F32, kind="ExternalInput").ap()
    bq_pp = nc.dram_tensor("bq_pp", [128, 8], F32, kind="ExternalInput").ap()
    bk_pp = nc.dram_tensor("bk_pp", [128, 8], F32, kind="ExternalInput").ap()
    bo_pp = nc.dram_tensor("bo_pp", [128, 8], F32, kind="ExternalInput").ap()
    bv_row = nc.dram_tensor("bv_row", [1, D], F32, kind="ExternalInput").ap()
    cq_t = nc.dram_tensor("cq_t", [128, 2048], F32, kind="ExternalInput").ap()
    sq_t = nc.dram_tensor("sq_t", [128, 2048], F32, kind="ExternalInput").ap()
    hselT_t = nc.dram_tensor("hselT_t", [8, 128], F32, kind="ExternalInput").ap()
    hsel128_t = nc.dram_tensor("hsel128_t", [128, 8], F32, kind="ExternalInput").ap()
    sched_t = nc.dram_tensor("sched_t", [1, 128], U32, kind="ExternalInput").ap()

    out_d = nc.dram_tensor("out", [D, 2048], F32, kind="ExternalOutput").ap()
    dbg_d = nc.dram_tensor("dbg_x1", [D, 2048], F32, kind="ExternalOutput").ap()
    dbg2_d = nc.dram_tensor("dbg_ffn", [D, 2048], F32, kind="ExternalOutput").ap()
    dbg_den = nc.dram_tensor("dbg_den", [16, 2048], F32, kind="ExternalOutput").ap()
    dbg_vb = nc.dram_tensor("dbg_vb", [256, 1536], BF16, kind="ExternalOutput").ap()
    dbg_sel = nc.dram_tensor("dbg_sel", [16, T], BF16, kind="ExternalOutput").ap()

    qa_d = nc.dram_tensor("qa_d", [D, 2048], BF16).ap()
    qb_d = nc.dram_tensor("qb_d", [D, 2048], BF16).ap()
    ka_g = nc.dram_tensor("ka_g", [D, TP], BF16).ap()
    va_g = nc.dram_tensor("va_g", [TP, VW], BF16).ap()
    kb_g = nc.dram_tensor("kb_g", [D, TP], BF16).ap()
    vb_g = nc.dram_tensor("vb_g", [TP, VW], BF16).ap()
    x1_d = nc.dram_tensor("x1_d", [D, 2048], F32).ap()
    s_glob = nc.dram_tensor("s_glob", [8, T], F32).ap()
    gates_d = nc.dram_tensor("gates_d", [8, T], F32).ap()
    selb_d = nc.dram_tensor("selb_d", [16, T], BF16).ap()
    agk_in = [nc.dram_tensor(f"agk_in{s}", [D, 2048], BF16).ap() for s in range(2)]
    agk_out = [nc.dram_tensor(f"agk_out{s}", [4 * D, 2048], BF16).ap() for s in range(2)]
    agv_in = [nc.dram_tensor(f"agv_in{s}", [2048, VW], BF16).ap() for s in range(2)]
    agv_out = [nc.dram_tensor(f"agv_out{s}", [4 * 2048, VW], BF16).ap() for s in range(2)]
    ags_in = nc.dram_tensor("ags_in", [8, 2048], F32).ap()
    ags_out = nc.dram_tensor("ags_out", [32, 2048], F32).ap()

    RG = [[0, 1, 2, 3], [4, 5, 6, 7]]
    preg = p.es.enter_context(nc.gpsimd.register("preg"))
    sreg = p.es.enter_context(nc.sync.register("sreg"))

    sb = p.sb
    w1_sb = sb("w1_sb", [128, 8 * 1024], BF16)
    wo_sb = sb("wo_sb", [128, 8 * 1024], BF16)
    wrt_sb = sb("wrt_sb", [128, 8 * 8], F32)
    bq_sb = sb("bq_sb", [128, 8], F32)
    bk_sb = sb("bk_sb", [128, 8], F32)
    bo_sb = sb("bo_sb", [128, 8], F32)
    bvr_sb = sb("bvr_sb", [128, 1024], BF16)
    cq_sb = sb("cq_sb", [128, 2048], BF16)
    sq_sb = sb("sq_sb", [128, 2048], BF16)
    hselT_sb = sb("hselT_sb", [8, 128], F32)
    hsel128_sb = sb("hsel128_sb", [128, 8], F32)
    sched_sb = sb("sched_sb", [1, 128], U32)
    ones_c = sb("ones_c", [128, 1], BF16)
    kconst = sb("kconst", [128, 3], F32)
    onesf_r = sb("onesf_r", [1, 128], F32)
    zero_big = sb("zero_big", [128, 1536], BF16)
    bv1_sb = sb("bv1_sb", [1, 1024], F32)

    xch = sb("xch", [128, 8 * 512], F32)
    lnbf = sb("lnbf", [128, 8 * 512], BF16)
    sqc = [sb(f"sqc{i}", [128, 512], BF16) for i in range(2)]
    mrow = sb("mrow", [1, 512], F32)
    vrow = sb("vrow", [1, 512], F32)
    rrow = sb("rrow", [1, 512], F32)
    ev = [sb(f"ev{i}", [128, 512], BF16) for i in range(2)]
    ropeC = sb("ropeC", [128, 512], BF16)
    ropeS = sb("ropeS", [128, 512], F32)
    ropeT = sb("ropeT", [128, 512], BF16)
    v97s = sb("v97s", [128, 1536], BF16)
    srow_sb = sb("srow_sb", [8, 512], F32)

    kg = [sb(f"kg{i}", [64, 4096], BF16) for i in range(2)]
    vg = [sb(f"vg{i}", [128, 8 * 384], BF16) for i in range(2)]
    qg = [sb(f"qg{i}", [64, 1024], BF16) for i in range(4)]
    pt = [sb(f"pt{i}", [128, 1024], BF16) for i in range(2)]
    pair_sb = sb("pair_sb", [128, 8 * 256], BF16)
    av_sb = sb("av_sb", [64, 256], BF16)
    den_sb = sb("den_sb", [1, 256], F32)
    grow = [sb(f"grow{i}", [1, 256], F32) for i in range(8)]
    aout = sb("aout", [128, 8 * 256], F32)
    xq = sb("xq", [128, 8 * 256], F32)
    emrow = sb("emrow", [1, 256], F32)
    evrow = sb("evrow", [1, 256], F32)
    errow = sb("errow", [1, 256], F32)

    bis_S = sb("bis_S", [128, 512], F32)
    bis_tmp = sb("bis_tmp", [128, 512], F32)
    bis_red = sb("bis_red", [128, 1], F32)
    bis_lo = sb("bis_lo", [8, 1], F32)
    bis_hi = sb("bis_hi", [8, 1], F32)
    bis_mid = sb("bis_mid", [8, 1], F32)
    bis_ge = sb("bis_ge", [8, 1], U32)
    selbf = sb("selbf", [128, 512], BF16)
    gts = sb("gts", [128, 512], F32)
    selT_sb = sb("selT_sb", [128, 16], BF16)
    selTf_sb = sb("selTf_sb", [128, 16], F32)
    vm_sb = [sb(f"vm_sb{i}", [128, 768], BF16) for i in range(2)]

    p.begin()
    p.on("pool", lambda g: g.dma_start(out=cq_sb[:], in_=cq_t), "gd", 16)
    p.on("pool", lambda g: g.dma_start(out=sq_sb[:], in_=sq_t), "gd", 16)
    for src, dst in ((w_rt, wrt_sb),):
        p.on("sp", lambda e, s=src, d=dst: e.dma_start(
            out=d[:].rearrange("p (c o) -> p c o", c=8),
            in_=s.rearrange("(c p) o -> p c o", p=128)), "ld", 16)
    for src, dst in ((bq_pp, bq_sb), (bk_pp, bk_sb), (bo_pp, bo_sb),
                     (hselT_t, hselT_sb), (hsel128_t, hsel128_sb),
                     (sched_t, sched_sb), (bv_row, bv1_sb)):
        p.on("sp", lambda e, s=src, d=dst: e.dma_start(out=d[:], in_=s), "ld", 16)
    p.on("dve", lambda v: v.memset(ones_c[:], 1.0), "dv")
    p.on("dve", lambda v: v.memset(kconst[:, 0:1], EPS), "dv")
    p.on("dve", lambda v: v.memset(kconst[:, 1:2], 1e-30), "dv")
    p.on("dve", lambda v: v.memset(kconst[:, 2:3], -SHIFT), "dv")
    p.on("dve", lambda v: v.memset(onesf_r[:], 1.0), "dv")
    p.on("dve", lambda v: v.memset(zero_big[:], 0.0), "dv")
    p.on("dve", lambda v: v.memset(v97s[:], 0.0), "dv")
    for h in range(16):
        p.on("dve", lambda v, h=h: v.memset(v97s[:, 96 * h + 64:96 * h + 65], 1.0), "dv")
    p.flush()

    p.begin()
    for kgl in (ka_g, kb_g):
        for r in range(8):
            p.on("sp", lambda e, k=kgl, r=r: e.dma_start(
                out=k[128 * r:128 * (r + 1), 0:PAD], in_=zero_big[:, 0:PAD]), "ld", 16)
    for vgl in (va_g, vb_g):
        for r in range(6):
            p.on("sp", lambda e, v=vgl, r=r: e.dma_start(
                out=v[128 * r:128 * (r + 1), :], in_=zero_big[:]), "ld", 16)
    p.flush()

    with nc.psum_tensor("pbv", [128, 1024], F32) as pbv:
        p.begin()
        p.wait("pe", "ld")
        p.on("pe", lambda t: t.matmul(pbv[:, 0:512], onesf_r[:], bv1_sb[0:1, 0:512],
                                      start=True, stop=True))
        p.on("pe", lambda t: t.matmul(pbv[:, 512:1024], onesf_r[:], bv1_sb[0:1, 512:1024],
                                      start=True, stop=True), "pe")
        p.wait("dve", "pe")
        p.on("dve", lambda v: v.tensor_copy(bvr_sb[:], pbv[:]), "dv")
        p.flush()

    def w_load(src, dst=None, rearr=True):
        dst = w1_sb if dst is None else dst
        p.begin()
        p.wait("pool", "pe")
        if rearr:
            ap = src.rearrange("(c p) o -> p c o", p=128)
            p.on("pool", lambda g, s=ap, d=dst: g.dma_start(
                out=d[:].rearrange("p (c o) -> p c o", c=8), in_=s), "gd", 16)
        else:
            p.on("pool", lambda g, s=src, d=dst: g.dma_start(out=d[:], in_=s), "gd", 16)
        p.flush()

    def emit_ln_block(src_f32, out_bf, n, psA, psB, psM, psR, also_f32=False):
        p.begin()
        p.wait("dve", "ld")
        p.wait("dve", "gd")
        p.wait("dve", "pe")
        p.wait("dve", "st")
        cast_dv = []
        for i in range(8):
            p.on("dve", lambda v, i=i: v.tensor_copy(
                out_bf[:, n * i:n * (i + 1)], src_f32[:, n * i:n * (i + 1)]), "dv")
            cast_dv.append(p.cnt("dv"))
        for i in range(8):
            p.wait("pe", "dv", cast_dv[i])
            p.on("pe", lambda t, i=i: t.matmul(
                psA[0:1, 0:n], ones_c[:], out_bf[:, n * i:n * (i + 1)],
                start=(i == 0), stop=(i == 7)), "pe" if i == 7 else None)
        sq_dv = []
        for i in range(8):
            if i >= 2:
                p.wait("dve", "pe", p.cnt("pe"))
            p.on("dve", lambda v, i=i: v.tensor_tensor(
                sqc[i % 2][:], out_bf[:, n * i:n * (i + 1)],
                out_bf[:, n * i:n * (i + 1)], ALU.mult), "dv")
            sq_dv.append(p.cnt("dv"))
        for i in range(8):
            p.wait("pe", "dv", sq_dv[i])
            p.on("pe", lambda t, i=i: t.matmul(
                psB[0:1, 0:n], ones_c[:], sqc[i % 2][:],
                start=(i == 0), stop=(i == 7)), "pe" if i == 7 else None)
        p.flush()
        p.begin()
        p.wait("dve", "pe")
        p.on("dve", lambda v: v.tensor_scalar(mrow[:, 0:n], psA[0:1, 0:n], 1.0 / D, None, ALU.mult))
        p.on("dve", lambda v: v.tensor_scalar(vrow[:, 0:n], psB[0:1, 0:n], 1.0 / D, None, ALU.mult))
        p.on("dve", lambda v: v.tensor_tensor(rrow[:, 0:n], mrow[:, 0:n], mrow[:, 0:n], ALU.mult))
        p.on("dve", lambda v: v.tensor_tensor(vrow[:, 0:n], vrow[:, 0:n], rrow[:, 0:n], ALU.subtract), "dv")
        p.wait("act", "dv")
        p.on("act", lambda s: s.activation(rrow[:, 0:n], vrow[:, 0:n], AF.Ln, bias=kconst[0:1, 0:1]))
        p.on("act", lambda s: s.activation(rrow[:, 0:n], rrow[:, 0:n], AF.Exp, scale=-0.5), "ex")
        p.wait("pe", "ex")
        p.on("pe", lambda t: t.matmul(psM[:, 0:n], onesf_r[:], mrow[0:1, 0:n], start=True, stop=True))
        p.on("pe", lambda t: t.matmul(psR[:, 0:n], onesf_r[:], rrow[0:1, 0:n], start=True, stop=True), "pe")
        p.flush()
        p.begin()
        p.wait("dve", "pe")
        for i in range(8):
            sl = slice(n * i, n * (i + 1))
            if also_f32:
                p.on("dve", lambda v, sl=sl: v.tensor_tensor(src_f32[:, sl], src_f32[:, sl], psM[:, 0:n], ALU.subtract))
                p.on("dve", lambda v, sl=sl: v.tensor_tensor(src_f32[:, sl], src_f32[:, sl], psR[:, 0:n], ALU.mult))
                p.on("dve", lambda v, sl=sl: v.tensor_copy(out_bf[:, sl], src_f32[:, sl]), "dv")
            else:
                p.on("dve", lambda v, sl=sl: v.tensor_tensor(out_bf[:, sl], out_bf[:, sl], psM[:, 0:n], ALU.subtract))
                p.on("dve", lambda v, sl=sl: v.tensor_tensor(out_bf[:, sl], out_bf[:, sl], psR[:, 0:n], ALU.mult), "dv")
        p.flush()

    def emit_proj_fm(dest, dest_r0, nout, bias_sb, rope, tb, pp, n=512):
        for oc in range(nout):
            p.begin()
            p.wait("pe", "gd")
            p.wait("pe", "dv")
            for dc in range(8):
                p.on("pe", lambda t, dc=dc, oc=oc: t.matmul(
                    pp[:, 0:n], w1_sb[:, 1024 * dc + 128 * oc:1024 * dc + 128 * (oc + 1)],
                    lnbf[:, n * dc:n * (dc + 1)],
                    start=(dc == 0), stop=(dc == 7)), "pe" if dc == 7 else None)
            p.wait("dve", "pe")
            p.wait("dve", "st")
            if rope:
                csl = slice(n * tb, n * (tb + 1))
                p.on("dve", lambda v, csl=csl: v.tensor_tensor(ropeC[:], pp[:, 0:n], cq_sb[:, csl], ALU.mult))
                p.on("dve", lambda v: v.stream_shuffle(ropeS[:], pp[:, 0:n], mask=SHUF))
                p.on("dve", lambda v, csl=csl: v.tensor_tensor(ropeT[:], ropeS[:], sq_sb[:, csl], ALU.mult))
                p.on("dve", lambda v: v.tensor_tensor(ropeC[:], ropeC[:], ropeT[:], ALU.add), "dv")
                src_t = ropeC
            else:
                if bias_sb is not None:
                    p.on("dve", lambda v, oc=oc: v.tensor_scalar(
                        ev[0][:], pp[:, 0:n], bias_sb[:, oc:oc + 1], None, ALU.add), "dv")
                else:
                    p.on("dve", lambda v: v.tensor_copy(ev[0][:], pp[:, 0:n]), "dv")
                src_t = ev[0]
            p.wait("act", "dv")
            p.on("act", lambda s, oc=oc, src_t=src_t: s.dma_start(
                out=dest[dest_r0 + 128 * oc:dest_r0 + 128 * (oc + 1), n * tb:n * (tb + 1)],
                in_=src_t[:]), "st", 16)
            p.flush()

    def emit_proj_v97(dest, withbias, tb, pp, n=512):
        for tt in range(n // 128):
            p.begin()
            p.wait("pe", "gd")
            p.wait("pe", "dv")
            p.wait("pe", "st", p.cnt("st"))
            for dc in range(8):
                for hh in range(2):
                    p.on("pe", lambda t, dc=dc, hh=hh, tt=tt: t.matmul(
                        pp[:, 512 * hh:512 * (hh + 1)],
                        lnbf[:, n * dc + 128 * tt:n * dc + 128 * (tt + 1)],
                        w1_sb[:, 1024 * dc + 512 * hh:1024 * dc + 512 * (hh + 1)],
                        start=(dc == 0), stop=(dc == 7)), "pe" if (dc == 7 and hh == 1) else None)
            p.wait("dve", "pe")
            p.wait("dve", "st")
            view = v97s[:].rearrange("p (h c) -> p h c", c=96)[:, :, 0:64]
            srcv = pp[:, 0:1024].rearrange("p (h c) -> p h c", c=64)
            if withbias:
                bsrc = bvr_sb[:].rearrange("p (h c) -> p h c", c=64)
                p.on("dve", lambda v: v.tensor_tensor(view, srcv, bsrc, ALU.add), "dv")
            else:
                p.on("dve", lambda v: v.tensor_copy(view, srcv), "dv")
            p.wait("act", "dv")
            p.on("act", lambda s, tt=tt: s.dma_start(
                out=dest[n * tb + 128 * tt:n * tb + 128 * (tt + 1), :],
                in_=v97s[:]), "st", 16)
            p.flush()

    def emit_stage_proj(stage):
        src = x_in if stage == 0 else x1_d
        with nc.psum_tensor(f"psA{stage}", [1, 512], F32) as psA, \
             nc.psum_tensor(f"psB{stage}", [1, 512], F32) as psB, \
             nc.psum_tensor(f"psM{stage}", [128, 512], F32) as psM, \
             nc.psum_tensor(f"psR{stage}", [128, 512], F32) as psR, \
             nc.psum_tensor(f"pp{stage}", [128, 1024], F32) as pp:
            for tb in range(4):
                p.begin()
                p.wait("sp", "dv")
                p.wait("sp", "pe")
                p.wait("sp", "st")
                p.on("sp", lambda e, tb=tb: e.dma_start(
                    out=xch[:],
                    in_=src.rearrange("(c p) n -> p c n", p=128)[:, :, 512 * tb:512 * (tb + 1)]),
                    "ld", 16)
                p.flush()
                emit_ln_block(xch, lnbf, 512, psA, psB, psM, psR, also_f32=(stage == 1))
                if stage == 1:
                    p.begin()
                    p.wait("pe", "dv")
                    for dc in range(8):
                        p.on("pe", lambda t, dc=dc: t.matmul(
                            pp[0:8, 0:512], wrt_sb[:, 8 * dc:8 * (dc + 1)],
                            xch[:, 512 * dc:512 * (dc + 1)],
                            start=(dc == 0), stop=(dc == 7)), "pe" if dc == 7 else None)
                    p.wait("dve", "pe")
                    p.wait("dve", "st")
                    p.on("dve", lambda v: v.tensor_copy(srow_sb[:], pp[0:8, 0:512]), "dv")
                    p.wait("act", "dv")
                    p.on("act", lambda s, tb=tb: s.dma_start(
                        out=ags_in[:, 512 * tb:512 * (tb + 1)], in_=srow_sb[:]), "st", 16)
                    p.flush()
                wq = w_q_a if stage == 0 else w_q_b
                wk = w_k_a if stage == 0 else w_k_b
                wv = w_v_a if stage == 0 else w_v_b
                qd = qa_d if stage == 0 else qb_d
                kin = agk_in[stage]
                vin = agv_in[stage]
                w_load(wq)
                emit_proj_fm(qd, 0, 8, bq_sb if stage == 0 else None, stage == 1, tb, pp)
                w_load(wk)
                emit_proj_fm(kin, 0, 8, bk_sb if stage == 0 else None, stage == 1, tb, pp)
                w_load(wv)
                emit_proj_v97(vin, stage == 0, tb, pp)

    def emit_ag(stage):
        p.begin()
        p.wait("pool", "st")
        p.on("pool", lambda g, s=stage: g.collective_compute(
            "AllGather", ALU.bypass, replica_groups=RG,
            ins=[agk_in[s].opt()], outs=[agk_out[s].opt()]), "cc")
        p.on("pool", lambda g, s=stage: g.collective_compute(
            "AllGather", ALU.bypass, replica_groups=RG,
            ins=[agv_in[s].opt()], outs=[agv_out[s].opt()]), "cc")
        if stage == 1:
            p.on("pool", lambda g: g.collective_compute(
                "AllGather", ALU.bypass, replica_groups=RG,
                ins=[ags_in.opt()], outs=[ags_out.opt()]), "cc")
        p.wait("pool", "cc")
        p.flush()
        kgl = ka_g if stage == 0 else kb_g
        vgl = va_g if stage == 0 else vb_g
        p.begin()
        for rr in range(4):
            for j in range(NQMT):
                dcol = PAD + 1024 * j + 256 * _sigma(rr, j)
                p.on("sp", lambda e, rr=rr, j=j, dcol=dcol, s=stage, k=kgl: e.dma_start(
                    out=k[:, dcol:dcol + 256],
                    in_=agk_out[s][1024 * rr:1024 * (rr + 1), 256 * j:256 * (j + 1)]), "ld", 16)
                p.on("sp", lambda e, rr=rr, j=j, dcol=dcol, s=stage, v=vgl: e.dma_start(
                    out=v[dcol:dcol + 256, :],
                    in_=agv_out[s][2048 * rr + 256 * j:2048 * rr + 256 * (j + 1), :]), "ld", 16)
                if stage == 1:
                    p.on("sp", lambda e, rr=rr, j=j, dcol=dcol: e.dma_start(
                        out=s_glob[:, dcol - PAD:dcol - PAD + 256],
                        in_=ags_out[8 * rr:8 * (rr + 1), 256 * j:256 * (j + 1)]), "ld", 16)
        p.flush()

    def emit_selection():
        with nc.psum_tensor("pcnt", [8, 1], F32) as pcnt, \
             nc.psum_tensor("ptau", [128, 1], F32) as ptau:
            p.begin()
            p.wait("sp", "ld")
            for h in range(8):
                p.on("sp", lambda e, h=h: e.dma_start(
                    out=bis_S[16 * h:16 * (h + 1), :],
                    in_=s_glob[h:h + 1, :].rearrange("o (s f) -> (o s) f", f=512)), "ld", 16)
            p.wait("dve", "ld")
            p.on("dve", lambda v: v.memset(bis_lo[:], -512.0))
            p.on("dve", lambda v: v.memset(bis_hi[:], 512.0), "dv")
            p.flush()
            for it in range(BITERS):
                p.begin()
                p.on("dve", lambda v: v.drain())
                p.on("dve", lambda v: v.tensor_tensor(bis_mid[:], bis_lo[:], bis_hi[:], ALU.add))
                p.on("dve", lambda v: v.drain())
                p.on("dve", lambda v: v.tensor_scalar(bis_mid[:], bis_mid[:], 0.5, None, ALU.mult))
                p.on("dve", lambda v: v.drain(), "dv")
                p.wait("pe", "dv")
                p.on("pe", lambda t: t.matmul(ptau[:], hselT_sb[:], bis_mid[:], start=True, stop=True), "pe")
                p.wait("dve", "pe")
                p.on("dve", lambda v: v.tensor_scalar(bis_tmp[:], bis_S[:], ptau[:, 0:1], None, ALU.is_ge))
                p.on("dve", lambda v: v.tensor_reduce(bis_red[:], bis_tmp[:], mybir.AxisListType.X, ALU.add))
                p.on("dve", lambda v: v.drain(), "dv")
                p.wait("pe", "dv")
                p.on("pe", lambda t: t.matmul(pcnt[:], hsel128_sb[:], bis_red[:], start=True, stop=True), "pe")
                p.wait("dve", "pe")
                p.on("dve", lambda v: v.tensor_scalar(bis_ge[:], pcnt[:], float(SEL_K), None, ALU.is_ge))
                p.on("dve", lambda v: v.drain())
                p.on("dve", lambda v: v.select(bis_lo[:], bis_ge[:], bis_mid[:], bis_lo[:]))
                p.on("dve", lambda v: v.select(bis_hi[:], bis_ge[:], bis_hi[:], bis_mid[:]), "dv")
                p.flush()
            p.begin()
            p.on("dve", lambda v: v.drain(), "dv")
            p.wait("pe", "dv")
            p.on("pe", lambda t: t.matmul(ptau[:], hselT_sb[:], bis_lo[:], start=True, stop=True), "pe")
            p.wait("dve", "pe")
            p.on("dve", lambda v: v.tensor_scalar(bis_tmp[:], bis_S[:], ptau[:, 0:1], None, ALU.is_ge))
            p.on("dve", lambda v: v.tensor_copy(selbf[:], bis_tmp[:]), "dv")
            p.wait("act", "dv")
            p.on("act", lambda s: s.activation(gts[:], bis_S[:], AF.Exp, scale=-1.0), "ex")
            p.wait("dve", "ex")
            p.on("dve", lambda v: v.tensor_scalar(gts[:], gts[:], 1.0, None, ALU.add), "dv")
            p.wait("act", "dv")
            p.on("act", lambda s: s.activation(gts[:], gts[:], AF.Ln))
            p.on("act", lambda s: s.activation(gts[:], gts[:], AF.Exp, scale=-1.0), "ex")
            p.wait("dve", "ex")
            p.on("dve", lambda v: v.tensor_tensor(gts[:], gts[:], bis_tmp[:], ALU.mult), "dv")
            p.wait("act", "dv")
            p.on("act", lambda s: s.dma_start(
                out=selb_d[0:8, :].rearrange("o (s f) -> (o s) f", f=512), in_=selbf[:]), "st", 16)
            p.on("act", lambda s: s.dma_start(
                out=gates_d.rearrange("o (s f) -> (o s) f", f=512), in_=gts[:]), "st", 16)
            p.on("act", lambda s: s.dma_start(
                out=selb_d[8:16, :].rearrange("o (s f) -> (o s) f", f=512), in_=selbf[:]), "st", 16)
            p.flush()
        if ABL_NOVM:
            return
        p.begin()
        p.wait("sp", "st")
        p.wait("sp", "ld")
        dv_done = [0, 0]
        st_done = [0, 0]
        for tt in range(64):
            b = tt % 2
            p.wait("sp", "dv", dv_done[b])
            p.wait("sp", "st", st_done[b])
            p.wait("sp", "dv")
            p.on("sp", lambda e, tt=tt, b=b: e.dma_start(
                out=vm_sb[b][:], in_=vb_g[PAD + 128 * tt:PAD + 128 * (tt + 1), 768:1536]), "ld", 16)
            p.on("sp", lambda e, tt=tt, b=b: e.dma_start(
                out=selT_sb[:], in_=selb_d[:, 128 * tt:128 * (tt + 1)], transpose=True), "ld", 16)
            p.wait("dve", "ld")
            p.wait("dve", "st", p.cnt("st"))
            p.on("dve", lambda v: v.tensor_copy(selTf_sb[:], selT_sb[:]))
            p.on("dve", lambda v: v.drain())
            for h in range(8):
                p.on("dve", lambda v, h=h, b=b: v.tensor_scalar(
                    vm_sb[b][:, 96 * h:96 * (h + 1)], vm_sb[b][:, 96 * h:96 * (h + 1)],
                    selTf_sb[:, h + 8:h + 9], None, ALU.mult), "dv" if h == 7 else None)
            dv_done[b] = p.cnt("dv")
            p.wait("act", "dv", dv_done[b])
            p.on("act", lambda s, tt=tt, b=b: s.dma_start(
                out=vb_g[PAD + 128 * tt:PAD + 128 * (tt + 1), 768:1536], in_=vm_sb[b][:]), "st", 16)
            st_done[b] = p.cnt("st")
        p.flush()

    def emit_attention(stage):
        kgl = ka_g if stage == 0 else kb_g
        vgl = va_g if stage == 0 else vb_g
        qd = qa_d if stage == 0 else qb_d
        with nc.psum_tensor(f"pscA{stage}", [128, 1024], F32) as pscA, \
             nc.psum_tensor(f"pscB{stage}", [128, 1024], F32) as pscB, \
             nc.psum_tensor(f"pav0{stage}", [96, 256], F32) as pav0, \
             nc.psum_tensor(f"pav1{stage}", [96, 256], F32) as pav1, \
             nc.psum_tensor(f"pav2{stage}", [96, 256], F32) as pav2, \
             nc.psum_tensor(f"pav3{stage}", [96, 256], F32) as pav3:
            psc = [pscA, pscB]
            pav = [pav0, pav1, pav2, pav3]
            pbc = pscB[0:64, 0:256]
            pout = pscA[0:128, 0:256]
            psum_r = pscB[0:1, 512:768]
            psq_r = pscB[0:1, 768:1024]
            pm_b = pscA[0:128, 256:512]
            pr_b = pscA[0:128, 512:768]

            gidx = [0]
            kvseq = [0]
            last_ex_of_buf = {0: 0, 1: 0}
            last_av_of_kv = {0: 0, 1: 0}
            pt_last_av = {0: 0, 1: 0}
            last_q_av = [0]

            p.begin()
            p.wait("sp", "st")
            p.wait("pool", "ld")
            p.wait("pool", "st")
            for j in range(NQMT):
                p.wait("sp", "av", last_q_av[0])
                p.wait("sp", "dv")
                for g4 in range(4):
                    p.on("sp", lambda e, j=j, g4=g4, qd=qd: e.dma_start(
                        out=qg[g4][:].rearrange("p (h n) -> p h n", h=4),
                        in_=qd.rearrange("(g h p) n -> g p h n", g=4, p=64)[g4][:, :, 256 * j:256 * (j + 1)]),
                        "ld", 16)
                xsrc = x_in if stage == 0 else x1_d
                p.on("sp", lambda e, j=j, xsrc=xsrc: e.dma_start(
                    out=xq[:], in_=xsrc.rearrange("(c p) n -> p c n", p=128)[:, :, 256 * j:256 * (j + 1)]),
                    "ld", 16)
                q_ld = p.cnt("ld")
                if stage == 1:
                    p.wait("sp", "dv")
                    def load_gates(g, j=j):
                            for h in range(8):
                                nb = len(nc.inst_names())
                                g.reg_load(sreg, sched_sb[0:1, 72 + j:73 + j])
                                g.reg_add(sreg, sreg, h * T)
                                g.dma_start(out=grow[h][:],
                                            in_=bass.AP(gates_d.tensor, sreg, [[T, 1], [1, 256]])
                                            ).then_inc(p.S["gd"], 16)
                                _free_dyn_regs(nc, nb)
                    p.C["gd"] += 16 * 8
                    p.on("sp", load_gates)
                gates_gd = p.cnt("gd")

                for hg in range(4):
                    p.wait("pe", "dv")
                    for mi, m in enumerate(range(j, -1, -1)):
                        u = j * (j + 1) // 2 + mi
                        b = kvseq[0] % 2
                        kvseq[0] += 1
                        p.wait("sp", "av", last_av_of_kv[b])

                        def load_kv(g, u=u, b=b, hg=hg, kgl=kgl, vgl=vgl):
                                nb = len(nc.inst_names())
                                g.reg_load(sreg, sched_sb[0:1, 2 * u:2 * u + 1])
                                g.reg_add(sreg, sreg, 256 * hg * TP)
                                g.dma_start(
                                    out=kg[b][:].rearrange("p (h n) -> p h n", h=4),
                                    in_=bass.AP(kgl.tensor, sreg, [[TP, 64], [64 * TP, 4], [1, 1024]])
                                ).then_inc(p.S["gd"], 16)
                                g.reg_load(sreg, sched_sb[0:1, 2 * u + 1:2 * u + 2])
                                g.reg_add(sreg, sreg, 384 * hg)
                                g.dma_start(
                                    out=vg[b][:].rearrange("p (s c) -> p s c", s=8),
                                    in_=bass.AP(vgl.tensor, sreg, [[VW, 128], [128 * VW, 8], [1, 384]])
                                ).then_inc(p.S["gd"], 16)
                                _free_dyn_regs(nc, nb)
                        p.C["gd"] += 32
                        p.on("sp", load_kv)
                        kv_gd = p.cnt("gd")

                        for h4 in range(4):
                            for grp in range(2):
                                gb = gidx[0] % 2
                                gidx[0] += 1
                                p.wait("pe", "ex", last_ex_of_buf[gb])
                                p.wait("pe", "gd", kv_gd)
                                p.wait("pe", "ld", q_ld)
                                for kk in range(4):
                                    vv = 4 * grp + kk
                                    p.on("pe", lambda t, gb=gb, b=b, hg=hg, h4=h4, vv=vv, kk=kk: t.matmul(
                                        psc[gb][:, 256 * kk:256 * (kk + 1)],
                                        kg[b][:, 1024 * h4 + 128 * vv:1024 * h4 + 128 * (vv + 1)],
                                        qg[hg][:, 256 * h4:256 * (h4 + 1)],
                                        start=True, stop=True), "pe" if kk == 3 else None)
                                sc_pe = p.cnt("pe")
                                p.wait("act", "pe", sc_pe)
                                p.wait("act", "av", pt_last_av[gb])
                                p.on("act", lambda s, gb=gb: s.activation(
                                    pt[gb][:], psc[gb][:], AF.Exp, bias=kconst[:, 2:3], scale=0.125), "ex")
                                last_ex_of_buf[gb] = p.cnt("ex")
                                ex_here = p.cnt("ex")
                                need_ms = None
                                if m == 0 and grp == 1:
                                    p.wait("pool", "ex", ex_here)
                                    p.on("pool", lambda g, gb=gb: g.affine_select(
                                        pt[gb][:, 512:768], pt[gb][:, 512:768], [[1, 256]],
                                        ALU.is_ge, 0.0, base=0, channel_multiplier=-1))
                                    p.on("pool", lambda g, gb=gb: g.affine_select(
                                        pt[gb][:, 768:1024], pt[gb][:, 768:1024], [[1, 256]],
                                        ALU.is_ge, 0.0, base=-128, channel_multiplier=-1), "ms")
                                    need_ms = p.cnt("ms")
                                if need_ms is not None:
                                    p.wait("pe", "ms", need_ms)
                                else:
                                    p.wait("pe", "ex", ex_here)
                                first = (mi == 0 and grp == 0)
                                last = (m == 0 and grp == 1)
                                for kk in range(4):
                                    vv = 4 * grp + kk
                                    p.on("pe", lambda t, gb=gb, b=b, h4=h4, vv=vv, kk=kk,
                                         first=first, last=last: t.matmul(
                                        pav[h4][0:65, :],
                                        vg[b][:].rearrange("p (s c) -> p s c", s=8)[:, vv, 96 * h4:96 * h4 + 65],
                                        pt[gb][:, 256 * kk:256 * (kk + 1)],
                                        start=(first and kk == 0), stop=(last and kk == 3)),
                                        "av" if kk == 3 else None)
                                pt_last_av[gb] = p.cnt("av")
                        last_av_of_kv[b] = p.cnt("av")

                    for h4 in range(4):
                        hglob = 4 * hg + h4
                        p.wait("act", "av")
                        p.wait("act", "pe")
                        p.wait("act", "dv")
                        p.on("act", lambda s, h4=h4: s.activation(
                            den_sb[:], pav[h4][64:65, :], AF.Ln, bias=kconst[0:1, 1:2]))
                        p.on("act", lambda s: s.activation(
                            den_sb[:], den_sb[:], AF.Exp, scale=-1.0), "ex")
                        if stage == 1 and hg >= 2 and not ABL_NOGATE:
                            p.wait("dve", "ex")
                            p.wait("dve", "gd", gates_gd)
                            p.on("dve", lambda v, hglob=hglob: v.tensor_tensor(
                                den_sb[:], den_sb[:], grow[hglob - 8][:], ALU.mult), "dv")
                            p.wait("pe", "dv")
                        else:
                            p.wait("pe", "ex")
                        p.on("pe", lambda t: t.matmul(
                            pbc, onesf_r[0:1, 0:64], den_sb[:], start=True, stop=True), "pe")
                        p.wait("dve", "pe")
                        p.on("dve", lambda v, h4=h4: v.tensor_copy(av_sb[:], pav[h4][0:64, :]))
                        p.on("dve", lambda v, hglob=hglob: v.tensor_tensor(
                            pair_sb[64 * (hglob % 2):64 * (hglob % 2) + 64,
                                    256 * (hglob // 2):256 * (hglob // 2 + 1)],
                            av_sb[:], pbc, ALU.mult), "dv")
                        p.wait("pe", "dv")

                p.wait("dve", "av")
                for oc in range(8):
                    p.wait("pe", "dv")
                    npc = 4 if (stage == 1 and ABL_NOMOSA) else 8
                    for pc in range(npc):
                        p.on("pe", lambda t, oc=oc, pc=pc, npc=npc: t.matmul(
                            pout, wo_sb[:, 1024 * pc + 128 * oc:1024 * pc + 128 * (oc + 1)],
                            pair_sb[:, 256 * pc:256 * (pc + 1)],
                            start=(pc == 0), stop=(pc == npc - 1)), "pe" if pc == npc - 1 else None)
                    p.wait("dve", "pe")
                    if stage == 0:
                        p.on("dve", lambda v, oc=oc: v.tensor_scalar(
                            aout[:, 256 * oc:256 * (oc + 1)], pout, bo_sb[:, oc:oc + 1], None, ALU.add), "dv")
                    else:
                        p.on("dve", lambda v, oc=oc: v.tensor_copy(
                            aout[:, 256 * oc:256 * (oc + 1)], pout), "dv")
                cast_dv = []
                for i in range(8):
                    p.on("dve", lambda v, i=i: v.tensor_copy(
                        pair_sb[:, 256 * i:256 * (i + 1)], aout[:, 256 * i:256 * (i + 1)]), "dv")
                    cast_dv.append(p.cnt("dv"))
                for i in range(8):
                    p.wait("pe", "dv", cast_dv[i])
                    p.on("pe", lambda t, i=i: t.matmul(
                        psum_r, ones_c[:], pair_sb[:, 256 * i:256 * (i + 1)],
                        start=(i == 0), stop=(i == 7)), "pe" if i == 7 else None)
                sq_dv = []
                sums_pe = p.cnt("pe")
                for i in range(8):
                    if i >= 2:
                        p.wait("dve", "pe", sums_pe)
                    p.on("dve", lambda v, i=i: v.tensor_tensor(
                        pt[i % 2][:, 0:256],
                        pair_sb[:, 256 * i:256 * (i + 1)],
                        pair_sb[:, 256 * i:256 * (i + 1)], ALU.mult), "dv")
                    sq_dv.append(p.cnt("dv"))
                for i in range(8):
                    p.wait("pe", "dv", sq_dv[i])
                    p.on("pe", lambda t, i=i: t.matmul(
                        psq_r, ones_c[:], pt[i % 2][:, 0:256],
                        start=(i == 0), stop=(i == 7)), "pe" if i == 7 else None)
                p.wait("act", "dv")
                if stage == 1:
                    p.on("act", lambda s, j=j: s.dma_start(
                        out=dbg2_d.rearrange("(c p) n -> p c n", p=128)[:, :, 256 * j:256 * (j + 1)],
                        in_=aout[:]), "st", 16)
                p.wait("dve", "pe")
                p.on("dve", lambda v: v.tensor_scalar(emrow[:], psum_r, 1.0 / D, None, ALU.mult))
                p.on("dve", lambda v: v.tensor_scalar(evrow[:], psq_r, 1.0 / D, None, ALU.mult))
                p.on("dve", lambda v: v.tensor_tensor(errow[:], emrow[:], emrow[:], ALU.mult))
                p.on("dve", lambda v: v.tensor_tensor(evrow[:], evrow[:], errow[:], ALU.subtract), "dv")
                p.wait("act", "dv")
                p.on("act", lambda s: s.activation(errow[:], evrow[:], AF.Ln, bias=kconst[0:1, 0:1]))
                p.on("act", lambda s: s.activation(errow[:], errow[:], AF.Exp, scale=-0.5), "ex")
                p.wait("pe", "ex")
                p.on("pe", lambda t: t.matmul(pm_b, onesf_r[:], emrow[0:1, :], start=True, stop=True))
                p.on("pe", lambda t: t.matmul(pr_b, onesf_r[:], errow[0:1, :], start=True, stop=True), "pe")
                p.wait("dve", "pe")
                p.wait("dve", "ld", q_ld)
                for i in range(8):
                    sl = slice(256 * i, 256 * (i + 1))
                    p.on("dve", lambda v, sl=sl: v.tensor_tensor(aout[:, sl], aout[:, sl], pm_b, ALU.subtract))
                    p.on("dve", lambda v, sl=sl: v.tensor_tensor(aout[:, sl], aout[:, sl], pr_b, ALU.mult))
                    p.on("dve", lambda v, sl=sl: v.tensor_tensor(xq[:, sl], xq[:, sl], aout[:, sl], ALU.add),
                         "dv" if i == 7 else None)
                p.wait("act", "dv")
                ddst = x1_d if stage == 0 else out_d
                p.on("act", lambda s, j=j, ddst=ddst: s.dma_start(
                    out=ddst.rearrange("(c p) n -> p c n", p=128)[:, :, 256 * j:256 * (j + 1)],
                    in_=xq[:]), "st", 16)
                if stage == 0:
                    p.on("act", lambda s, j=j: s.dma_start(
                        out=dbg_d.rearrange("(c p) n -> p c n", p=128)[:, :, 256 * j:256 * (j + 1)],
                        in_=xq[:]), "st", 16)
                last_q_av[0] = p.cnt("av")
                p.wait("pe", "dv")
            p.flush()

    emit_stage_proj(0)
    emit_ag(0)
    w_load(w_o_a, wo_sb)
    emit_attention(0)
    emit_stage_proj(1)
    emit_ag(1)
    emit_selection()
    w_load(w_o_b, wo_sb)
    emit_attention(1)

    p.begin()
    p.wait("sp", "st")
    p.on("sp", lambda e: e.dma_start(out=dbg_vb[0:128, :], in_=vb_g[PAD:PAD + 128, :]), "ld", 16)
    p.on("sp", lambda e: e.dma_start(out=dbg_vb[128:256, :], in_=va_g[PAD:PAD + 128, :]), "ld", 16)
    p.on("sp", lambda e: e.dma_start(out=dbg_sel, in_=selb_d), "ld", 16)
    p.wait("sp", "ld")
    p.flush()

    p.es.close()
    return nc


def _mk_sched(ci):
    sched = np.zeros(128, dtype=np.uint32)
    u = 0
    for j in range(NQMT):
        q0 = 1024 * j + 256 * _sigma(ci, j)
        for mi, m in enumerate(range(j, -1, -1)):
            wincol = PAD + q0 + 256 - 1024 * (m + 1)
            assert 0 <= wincol <= TP - 1024
            sched[2 * u] = wincol
            sched[2 * u + 1] = wincol * VW
            u += 1
        sched[72 + j] = q0
    return sched.reshape(1, 128)


def _rope_tables(ci):
    pos = _own_cols(ci).astype(np.float32)
    inv = (np.float32(10000.0) ** (-(np.arange(16, dtype=np.float32)) / np.float32(16.0)))
    ang = pos[None, :] * inv[:, None]
    cosv = np.cos(ang).astype(np.float32)
    sinv = np.sin(ang).astype(np.float32)
    C = np.zeros((128, 2048), np.float32)
    S = np.zeros((128, 2048), np.float32)
    for blk in range(2):
        r = 64 * blk
        C[r + 0:r + 16] = cosv
        C[r + 16:r + 32] = cosv
        C[r + 32:r + 64] = 1.0
        S[r + 0:r + 16] = -sinv
        S[r + 16:r + 32] = sinv
        S[r + 32:r + 64] = 0.0
    return C, S


def kernel(L0_attn_encodings, L0_attn_attn_W, L0_attn_attn_b, L0_attn_out_W,
           L0_attn_out_b, L0_ffn_mosa_router_w, L0_ffn_mosa_wqkv, L0_ffn_mosa_wo,
           L0_ffn_dense_wqkv, L0_ffn_dense_wo):
    x = np.asarray(L0_attn_encodings)
    attn_W = np.asarray(L0_attn_attn_W)
    attn_b = np.asarray(L0_attn_attn_b)
    out_W = np.asarray(L0_attn_out_W)
    out_b = np.asarray(L0_attn_out_b)
    rt_w = np.asarray(L0_ffn_mosa_router_w)
    m_wqkv = np.asarray(L0_ffn_mosa_wqkv)
    m_wo = np.asarray(L0_ffn_mosa_wo)
    d_wqkv = np.asarray(L0_ffn_dense_wqkv)
    d_wo = np.asarray(L0_ffn_dense_wo)

    qcols = np.concatenate([np.arange(192 * h, 192 * h + 64) for h in range(NH_A)])
    kcols = qcols + 64
    vcols = qcols + 128
    w_q_a = np.ascontiguousarray(attn_W[:, qcols])
    w_k_a = np.ascontiguousarray(attn_W[:, kcols])
    w_v_a = np.ascontiguousarray(attn_W[:, vcols])
    bq = attn_b[qcols]; bk = attn_b[kcols]; bv = attn_b[vcols]
    w_q_b = np.ascontiguousarray(np.concatenate(
        [d_wqkv[:, :, 0:64].reshape(D, 512), m_wqkv[:, :, 0:64].reshape(D, 512)], axis=1))
    w_k_b = np.ascontiguousarray(np.concatenate(
        [d_wqkv[:, :, 64:128].reshape(D, 512), m_wqkv[:, :, 64:128].reshape(D, 512)], axis=1))
    w_v_b = np.ascontiguousarray(np.concatenate(
        [d_wqkv[:, :, 128:192].reshape(D, 512), m_wqkv[:, :, 128:192].reshape(D, 512)], axis=1))
    w_o_b = np.ascontiguousarray(np.concatenate([d_wo, m_wo], axis=0))

    hselT = np.zeros((8, 128), np.float32)
    for h in range(8):
        hselT[h, 16 * h:16 * (h + 1)] = 1.0
    hsel128 = np.ascontiguousarray(hselT.T)

    nc = build_program()
    in_maps = []
    for c in range(NCORE):
        bc, ci = c // 4, c % 4
        cols = _own_cols(ci)
        xc = np.ascontiguousarray(x[bc].T[:, cols])
        C, S = _rope_tables(ci)
        in_maps.append({
            "x": xc,
            "w_q_a": w_q_a, "w_k_a": w_k_a, "w_v_a": w_v_a, "w_o_a": out_W,
            "w_q_b": w_q_b, "w_k_b": w_k_b, "w_v_b": w_v_b, "w_o_b": w_o_b,
            "w_rt": np.ascontiguousarray(rt_w),
            "bq_pp": np.ascontiguousarray(bq.reshape(8, 128).T),
            "bk_pp": np.ascontiguousarray(bk.reshape(8, 128).T),
            "bo_pp": np.ascontiguousarray(out_b.reshape(8, 128).T),
            "bv_row": bv.reshape(1, D),
            "cq_t": C, "sq_t": S,
            "hselT_t": hselT, "hsel128_t": hsel128,
            "sched_t": _mk_sched(ci),
        })
    res = run_bass_kernel_spmd(nc, in_maps, core_ids=list(range(NCORE)))
    out = np.zeros((B, T, D), np.float32)
    for c in range(NCORE):
        bc, ci = c // 4, c % 4
        cols = _own_cols(ci)
        out[bc, cols, :] = np.asarray(res.results[c]["out"]).T
    return out


# version 10
# speedup vs baseline: 1.2909x; 1.2909x over previous
import numpy as np
import ml_dtypes
from contextlib import ExitStack
import concourse.bass as bass
import concourse.mybir as mybir
from concourse.bass_utils import run_bass_kernel_spmd

F32 = mybir.dt.float32
BF16 = mybir.dt.bfloat16
U32 = mybir.dt.uint32
AF = mybir.ActivationFunctionType
ALU = mybir.AluOpType

B, T, D = 2, 8192, 1024
NCORE = 8
NH_A, NH_B, HS = 16, 8, 64
PAD = 768
TP = PAD + T
QMT = 256
NQMT = 8
SHIFT = 20.0
EPS = 1e-5
SEL_K = 1024
BITERS = 46
VW = 16 * 96
SHUF = [(i + 16) % 32 for i in range(32)]
ABL_NOGATE = False
ABL_NOMOSA = False
ABL_NOVM = False
ABL_DUMPDEN = True


def _sigma(ci, j):
    return (ci + j) % 4


def _own_cols(ci):
    idx = []
    for j in range(NQMT):
        base = 1024 * j + 256 * _sigma(ci, j)
        idx.extend(range(base, base + 256))
    return np.asarray(idx, dtype=np.int64)


class PB:
    ENGS = ("sp", "act", "dve", "pool", "pe")

    def __init__(self, nc):
        self.nc = nc
        self.es = ExitStack()
        self.C = {}
        self.S = {}
        for name in ("ld", "st", "gd", "cc", "pe", "ex", "dv", "ms", "av"):
            self.S[name] = self.es.enter_context(nc.semaphore("s_" + name))
            self.C[name] = 0
        self.ops = None

    def sb(self, nm, shp, dt):
        return self.es.enter_context(self.nc.sbuf_tensor(nm, shp, dt))

    def begin(self):
        assert self.ops is None
        self.ops = {e: [] for e in self.ENGS}

    def flush(self):
        ops, self.ops = self.ops, None
        nc = self.nc
        with nc.Block() as blk:
            if ops["sp"]:
                blk.sync(lambda e, _o=ops["sp"]: [f(e) for f in _o])
            if ops["pe"]:
                blk.tensor(lambda e, _o=ops["pe"]: [f(e) for f in _o])
            if ops["act"]:
                blk.scalar(lambda e, _o=ops["act"]: [f(e) for f in _o])
            if ops["dve"]:
                blk.vector(lambda e, _o=ops["dve"]: [f(e) for f in _o])
            if ops["pool"]:
                blk.gpsimd(lambda e, _o=ops["pool"]: [f(e) for f in _o])

    def on(self, eng, fn, sem=None, n=1):
        if sem is not None:
            self.C[sem] += n
            s, k = self.S[sem], n

            def wrapped(e, _fn=fn, _s=s, _k=k):
                ins = _fn(e)
                ins.then_inc(_s, _k)
            self.ops[eng].append(wrapped)
        else:
            self.ops[eng].append(fn)

    def wait(self, eng, sem, val=None):
        v = self.C[sem] if val is None else val
        if v <= 0:
            return
        s = self.S[sem]
        self.ops[eng].append(lambda e, _s=s, _v=v: e.wait_ge(_s, _v))

    def cnt(self, sem):
        return self.C[sem]



_ENG_PREFIX = {"Pool": mybir.EngineType.Pool, "SP": mybir.EngineType.SP,
               "Activation": mybir.EngineType.Activation}


def _free_dyn_regs(nc, nb):
    regs = set()
    for nm in nc.inst_names()[nb:]:
        i2 = nc.find_inst_by_name(nm)
        if i2 is None:
            continue
        ra = i2.regs_accessed() if callable(i2.regs_accessed) else i2.regs_accessed
        for tup in ra:
            for rn in (tup if isinstance(tup, tuple) else (tup,)):
                if '_tmp_' in rn or '_snap_' in rn:
                    regs.add(rn)
    for rn in regs:
        eng = _ENG_PREFIX[rn.split('_', 1)[0]]
        nc.free_register(bass.RegisterHandle(rn, eng))


def build_program():
    nc = bass.Bass("TRN2", target_bir_lowering=False)
    p = PB(nc)

    x_in = nc.dram_tensor("x", [D, 2048], F32, kind="ExternalInput").ap()
    w_q_a = nc.dram_tensor("w_q_a", [D, D], F32, kind="ExternalInput").ap()
    w_k_a = nc.dram_tensor("w_k_a", [D, D], F32, kind="ExternalInput").ap()
    w_v_a = nc.dram_tensor("w_v_a", [D, D], F32, kind="ExternalInput").ap()
    w_o_a = nc.dram_tensor("w_o_a", [D, D], F32, kind="ExternalInput").ap()
    w_q_b = nc.dram_tensor("w_q_b", [D, D], F32, kind="ExternalInput").ap()
    w_k_b = nc.dram_tensor("w_k_b", [D, D], F32, kind="ExternalInput").ap()
    w_v_b = nc.dram_tensor("w_v_b", [D, D], F32, kind="ExternalInput").ap()
    w_o_b = nc.dram_tensor("w_o_b", [D, D], F32, kind="ExternalInput").ap()
    w_rt = nc.dram_tensor("w_rt", [D, 8], F32, kind="ExternalInput").ap()
    bq_pp = nc.dram_tensor("bq_pp", [128, 8], F32, kind="ExternalInput").ap()
    bk_pp = nc.dram_tensor("bk_pp", [128, 8], F32, kind="ExternalInput").ap()
    bo_pp = nc.dram_tensor("bo_pp", [128, 8], F32, kind="ExternalInput").ap()
    bv_row = nc.dram_tensor("bv_row", [1, D], F32, kind="ExternalInput").ap()
    cq_t = nc.dram_tensor("cq_t", [128, 2048], F32, kind="ExternalInput").ap()
    sq_t = nc.dram_tensor("sq_t", [128, 2048], F32, kind="ExternalInput").ap()
    hselT_t = nc.dram_tensor("hselT_t", [8, 128], F32, kind="ExternalInput").ap()
    hsel128_t = nc.dram_tensor("hsel128_t", [128, 8], F32, kind="ExternalInput").ap()
    sched_t = nc.dram_tensor("sched_t", [1, 128], U32, kind="ExternalInput").ap()

    out_d = nc.dram_tensor("out", [D, 2048], F32, kind="ExternalOutput").ap()
    dbg_d = nc.dram_tensor("dbg_x1", [D, 2048], F32, kind="ExternalOutput").ap()
    dbg2_d = nc.dram_tensor("dbg_ffn", [D, 2048], F32, kind="ExternalOutput").ap()
    dbg_den = nc.dram_tensor("dbg_den", [16, 2048], F32, kind="ExternalOutput").ap()
    dbg_vb = nc.dram_tensor("dbg_vb", [256, 1536], BF16, kind="ExternalOutput").ap()

    qa_d = nc.dram_tensor("qa_d", [D, 2048], BF16).ap()
    qb_d = nc.dram_tensor("qb_d", [D, 2048], BF16).ap()
    ka_g = nc.dram_tensor("ka_g", [D, TP], BF16).ap()
    va_g = nc.dram_tensor("va_g", [TP, VW], BF16).ap()
    kb_g = nc.dram_tensor("kb_g", [D, TP], BF16).ap()
    vb_g = nc.dram_tensor("vb_g", [TP, VW], BF16).ap()
    x1_d = nc.dram_tensor("x1_d", [D, 2048], F32).ap()
    s_glob = nc.dram_tensor("s_glob", [8, T], F32).ap()
    gates_d = nc.dram_tensor("gates_d", [8, T], F32).ap()
    selb_d = nc.dram_tensor("selb_d", [16, T], BF16).ap()
    agk_in = [nc.dram_tensor(f"agk_in{s}", [D, 2048], BF16).ap() for s in range(2)]
    agk_out = [nc.dram_tensor(f"agk_out{s}", [4 * D, 2048], BF16).ap() for s in range(2)]
    agv_in = [nc.dram_tensor(f"agv_in{s}", [2048, VW], BF16).ap() for s in range(2)]
    agv_out = [nc.dram_tensor(f"agv_out{s}", [4 * 2048, VW], BF16).ap() for s in range(2)]
    ags_in = nc.dram_tensor("ags_in", [8, 2048], F32).ap()
    ags_out = nc.dram_tensor("ags_out", [32, 2048], F32).ap()

    RG = [[0, 1, 2, 3], [4, 5, 6, 7]]
    preg = p.es.enter_context(nc.gpsimd.register("preg"))
    sreg = p.es.enter_context(nc.sync.register("sreg"))

    sb = p.sb
    w1_sb = sb("w1_sb", [128, 8 * 1024], BF16)
    wo_sb = sb("wo_sb", [128, 8 * 1024], BF16)
    wrt_sb = sb("wrt_sb", [128, 8 * 8], F32)
    bq_sb = sb("bq_sb", [128, 8], F32)
    bk_sb = sb("bk_sb", [128, 8], F32)
    bo_sb = sb("bo_sb", [128, 8], F32)
    bvr_sb = sb("bvr_sb", [128, 1024], BF16)
    cq_sb = sb("cq_sb", [128, 2048], BF16)
    sq_sb = sb("sq_sb", [128, 2048], BF16)
    hselT_sb = sb("hselT_sb", [8, 128], F32)
    hsel128_sb = sb("hsel128_sb", [128, 8], F32)
    sched_sb = sb("sched_sb", [1, 128], U32)
    ones_c = sb("ones_c", [128, 1], BF16)
    kconst = sb("kconst", [128, 3], F32)
    onesf_r = sb("onesf_r", [1, 128], F32)
    zero_big = sb("zero_big", [128, 1536], BF16)
    bv1_sb = sb("bv1_sb", [1, 1024], F32)

    xch = sb("xch", [128, 8 * 512], F32)
    lnbf = sb("lnbf", [128, 8 * 512], BF16)
    sqc = [sb(f"sqc{i}", [128, 512], BF16) for i in range(2)]
    mrow = sb("mrow", [1, 512], F32)
    vrow = sb("vrow", [1, 512], F32)
    rrow = sb("rrow", [1, 512], F32)
    ev = [sb(f"ev{i}", [128, 512], BF16) for i in range(2)]
    ropeC = sb("ropeC", [128, 512], BF16)
    ropeS = sb("ropeS", [128, 512], F32)
    ropeT = sb("ropeT", [128, 512], BF16)
    v97s = sb("v97s", [128, 1536], BF16)
    srow_sb = sb("srow_sb", [8, 512], F32)

    kg = [sb(f"kg{i}", [64, 4096], BF16) for i in range(2)]
    vg = [sb(f"vg{i}", [128, 8 * 384], BF16) for i in range(2)]
    qg = [sb(f"qg{i}", [64, 1024], BF16) for i in range(4)]
    pt = [sb(f"pt{i}", [128, 1024], BF16) for i in range(2)]
    pair_sb = sb("pair_sb", [128, 8 * 256], BF16)
    av_sb = sb("av_sb", [64, 256], BF16)
    den_sb = sb("den_sb", [1, 256], F32)
    grow = [sb(f"grow{i}", [1, 256], F32) for i in range(8)]
    aout = sb("aout", [128, 8 * 256], F32)
    xq = sb("xq", [128, 8 * 256], F32)
    emrow = sb("emrow", [1, 256], F32)
    evrow = sb("evrow", [1, 256], F32)
    errow = sb("errow", [1, 256], F32)

    bis_S = sb("bis_S", [128, 512], F32)
    bis_tmp = sb("bis_tmp", [128, 512], F32)
    bis_red = sb("bis_red", [128, 1], F32)
    bis_lo = sb("bis_lo", [8, 1], F32)
    bis_hi = sb("bis_hi", [8, 1], F32)
    bis_mid = sb("bis_mid", [8, 1], F32)
    bis_ge = sb("bis_ge", [8, 1], U32)
    selbf = sb("selbf", [128, 512], BF16)
    gts = sb("gts", [128, 512], F32)
    selT_sb = sb("selT_sb", [128, 16], BF16)
    selTf_sb = sb("selTf_sb", [128, 16], F32)
    vm_sb = [sb(f"vm_sb{i}", [128, 768], BF16) for i in range(2)]

    p.begin()
    p.on("pool", lambda g: g.dma_start(out=cq_sb[:], in_=cq_t), "gd", 16)
    p.on("pool", lambda g: g.dma_start(out=sq_sb[:], in_=sq_t), "gd", 16)
    for src, dst in ((w_rt, wrt_sb),):
        p.on("sp", lambda e, s=src, d=dst: e.dma_start(
            out=d[:].rearrange("p (c o) -> p c o", c=8),
            in_=s.rearrange("(c p) o -> p c o", p=128)), "ld", 16)
    for src, dst in ((bq_pp, bq_sb), (bk_pp, bk_sb), (bo_pp, bo_sb),
                     (hselT_t, hselT_sb), (hsel128_t, hsel128_sb),
                     (sched_t, sched_sb), (bv_row, bv1_sb)):
        p.on("sp", lambda e, s=src, d=dst: e.dma_start(out=d[:], in_=s), "ld", 16)
    p.on("dve", lambda v: v.memset(ones_c[:], 1.0), "dv")
    p.on("dve", lambda v: v.memset(kconst[:, 0:1], EPS), "dv")
    p.on("dve", lambda v: v.memset(kconst[:, 1:2], 1e-30), "dv")
    p.on("dve", lambda v: v.memset(kconst[:, 2:3], -SHIFT), "dv")
    p.on("dve", lambda v: v.memset(onesf_r[:], 1.0), "dv")
    p.on("dve", lambda v: v.memset(zero_big[:], 0.0), "dv")
    p.on("dve", lambda v: v.memset(v97s[:], 0.0), "dv")
    for h in range(16):
        p.on("dve", lambda v, h=h: v.memset(v97s[:, 96 * h + 64:96 * h + 65], 1.0), "dv")
    p.flush()

    p.begin()
    for kgl in (ka_g, kb_g):
        for r in range(8):
            p.on("sp", lambda e, k=kgl, r=r: e.dma_start(
                out=k[128 * r:128 * (r + 1), 0:PAD], in_=zero_big[:, 0:PAD]), "ld", 16)
    for vgl in (va_g, vb_g):
        for r in range(6):
            p.on("sp", lambda e, v=vgl, r=r: e.dma_start(
                out=v[128 * r:128 * (r + 1), :], in_=zero_big[:]), "ld", 16)
    p.flush()

    with nc.psum_tensor("pbv", [128, 1024], F32) as pbv:
        p.begin()
        p.wait("pe", "ld")
        p.on("pe", lambda t: t.matmul(pbv[:, 0:512], onesf_r[:], bv1_sb[0:1, 0:512],
                                      start=True, stop=True))
        p.on("pe", lambda t: t.matmul(pbv[:, 512:1024], onesf_r[:], bv1_sb[0:1, 512:1024],
                                      start=True, stop=True), "pe")
        p.wait("dve", "pe")
        p.on("dve", lambda v: v.tensor_copy(bvr_sb[:], pbv[:]), "dv")
        p.flush()

    def w_load(src, dst=None, rearr=True):
        dst = w1_sb if dst is None else dst
        p.begin()
        p.wait("pool", "pe")
        if rearr:
            ap = src.rearrange("(c p) o -> p c o", p=128)
            p.on("pool", lambda g, s=ap, d=dst: g.dma_start(
                out=d[:].rearrange("p (c o) -> p c o", c=8), in_=s), "gd", 16)
        else:
            p.on("pool", lambda g, s=src, d=dst: g.dma_start(out=d[:], in_=s), "gd", 16)
        p.flush()

    def emit_ln_block(src_f32, out_bf, n, psA, psB, psM, psR, also_f32=False):
        p.begin()
        p.wait("dve", "ld")
        p.wait("dve", "gd")
        p.wait("dve", "pe")
        p.wait("dve", "st")
        p.wait("dve", "ex")
        cast_dv = []
        for i in range(8):
            p.on("dve", lambda v, i=i: v.tensor_copy(
                out_bf[:, n * i:n * (i + 1)], src_f32[:, n * i:n * (i + 1)]), "dv")
            cast_dv.append(p.cnt("dv"))
        for i in range(8):
            p.wait("pe", "dv", cast_dv[i])
            p.on("pe", lambda t, i=i: t.matmul(
                psA[0:1, 0:n], ones_c[:], out_bf[:, n * i:n * (i + 1)],
                start=(i == 0), stop=(i == 7)), "pe" if i == 7 else None)
        sq_dv = []
        for i in range(8):
            if i >= 2:
                p.wait("dve", "pe", p.cnt("pe"))
            p.on("dve", lambda v, i=i: v.tensor_tensor(
                sqc[i % 2][:], out_bf[:, n * i:n * (i + 1)],
                out_bf[:, n * i:n * (i + 1)], ALU.mult), "dv")
            sq_dv.append(p.cnt("dv"))
        for i in range(8):
            p.wait("pe", "dv", sq_dv[i])
            p.on("pe", lambda t, i=i: t.matmul(
                psB[0:1, 0:n], ones_c[:], sqc[i % 2][:],
                start=(i == 0), stop=(i == 7)), "pe" if i == 7 else None)
        p.flush()
        p.begin()
        p.wait("dve", "pe")
        p.on("dve", lambda v: v.tensor_scalar(mrow[:, 0:n], psA[0:1, 0:n], 1.0 / D, None, ALU.mult))
        p.on("dve", lambda v: v.tensor_scalar(vrow[:, 0:n], psB[0:1, 0:n], 1.0 / D, None, ALU.mult))
        p.on("dve", lambda v: v.tensor_tensor(rrow[:, 0:n], mrow[:, 0:n], mrow[:, 0:n], ALU.mult))
        p.on("dve", lambda v: v.tensor_tensor(vrow[:, 0:n], vrow[:, 0:n], rrow[:, 0:n], ALU.subtract), "dv")
        p.wait("act", "dv")
        p.on("act", lambda s: s.activation(rrow[:, 0:n], vrow[:, 0:n], AF.Ln, bias=kconst[0:1, 0:1]))
        p.on("act", lambda s: s.activation(rrow[:, 0:n], rrow[:, 0:n], AF.Exp, scale=-0.5), "ex")
        p.wait("pe", "ex")
        p.on("pe", lambda t: t.matmul(psM[:, 0:n], onesf_r[:], mrow[0:1, 0:n], start=True, stop=True))
        p.on("pe", lambda t: t.matmul(psR[:, 0:n], onesf_r[:], rrow[0:1, 0:n], start=True, stop=True), "pe")
        p.wait("dve", "pe")
        for i in range(8):
            sl = slice(n * i, n * (i + 1))
            if also_f32:
                p.on("dve", lambda v, sl=sl: v.tensor_tensor(src_f32[:, sl], src_f32[:, sl], psM[:, 0:n], ALU.subtract))
                p.on("dve", lambda v, sl=sl: v.tensor_tensor(src_f32[:, sl], src_f32[:, sl], psR[:, 0:n], ALU.mult))
                p.on("dve", lambda v, sl=sl: v.tensor_copy(out_bf[:, sl], src_f32[:, sl]), "dv")
            else:
                p.on("dve", lambda v, sl=sl: v.tensor_tensor(out_bf[:, sl], out_bf[:, sl], psM[:, 0:n], ALU.subtract))
                p.on("dve", lambda v, sl=sl: v.tensor_tensor(out_bf[:, sl], out_bf[:, sl], psR[:, 0:n], ALU.mult), "dv")
        p.flush()

    def emit_proj_fm(dest, dest_r0, nout, bias_sb, rope, tb, pp, n=512):
        p.begin()
        for oc in range(nout):
            p.wait("pe", "gd")
            p.wait("pe", "dv")
            for dc in range(8):
                p.on("pe", lambda t, dc=dc, oc=oc: t.matmul(
                    pp[:, 0:n], w1_sb[:, 1024 * dc + 128 * oc:1024 * dc + 128 * (oc + 1)],
                    lnbf[:, n * dc:n * (dc + 1)],
                    start=(dc == 0), stop=(dc == 7)), "pe" if dc == 7 else None)
            p.wait("dve", "pe")
            p.wait("dve", "st")
            if rope:
                csl = slice(n * tb, n * (tb + 1))
                p.on("dve", lambda v, csl=csl: v.tensor_tensor(ropeC[:], pp[:, 0:n], cq_sb[:, csl], ALU.mult))
                p.on("dve", lambda v: v.stream_shuffle(ropeS[:], pp[:, 0:n], mask=SHUF))
                p.on("dve", lambda v, csl=csl: v.tensor_tensor(ropeT[:], ropeS[:], sq_sb[:, csl], ALU.mult))
                p.on("dve", lambda v: v.tensor_tensor(ropeC[:], ropeC[:], ropeT[:], ALU.add), "dv")
                src_t = ropeC
            else:
                if bias_sb is not None:
                    p.on("dve", lambda v, oc=oc: v.tensor_scalar(
                        ev[0][:], pp[:, 0:n], bias_sb[:, oc:oc + 1], None, ALU.add), "dv")
                else:
                    p.on("dve", lambda v: v.tensor_copy(ev[0][:], pp[:, 0:n]), "dv")
                src_t = ev[0]
            p.wait("act", "dv")
            p.on("act", lambda s, oc=oc, src_t=src_t: s.dma_start(
                out=dest[dest_r0 + 128 * oc:dest_r0 + 128 * (oc + 1), n * tb:n * (tb + 1)],
                in_=src_t[:]), "st", 16)
        p.flush()

    def emit_proj_v97(dest, withbias, tb, pp, n=512):
        p.begin()
        for tt in range(n // 128):
            p.wait("pe", "gd")
            p.wait("pe", "dv")
            for dc in range(8):
                for hh in range(2):
                    p.on("pe", lambda t, dc=dc, hh=hh, tt=tt: t.matmul(
                        pp[:, 512 * hh:512 * (hh + 1)],
                        lnbf[:, n * dc + 128 * tt:n * dc + 128 * (tt + 1)],
                        w1_sb[:, 1024 * dc + 512 * hh:1024 * dc + 512 * (hh + 1)],
                        start=(dc == 0), stop=(dc == 7)), "pe" if (dc == 7 and hh == 1) else None)
            p.wait("dve", "pe")
            p.wait("dve", "st")
            view = v97s[:].rearrange("p (h c) -> p h c", c=96)[:, :, 0:64]
            srcv = pp[:, 0:1024].rearrange("p (h c) -> p h c", c=64)
            if withbias:
                bsrc = bvr_sb[:].rearrange("p (h c) -> p h c", c=64)
                p.on("dve", lambda v: v.tensor_tensor(view, srcv, bsrc, ALU.add), "dv")
            else:
                p.on("dve", lambda v: v.tensor_copy(view, srcv), "dv")
            p.wait("act", "dv")
            p.on("act", lambda s, tt=tt: s.dma_start(
                out=dest[n * tb + 128 * tt:n * tb + 128 * (tt + 1), :],
                in_=v97s[:]), "st", 16)
        p.flush()

    def emit_stage_proj(stage):
        src = x_in if stage == 0 else x1_d
        with nc.psum_tensor(f"psA{stage}", [1, 512], F32) as psA, \
             nc.psum_tensor(f"psB{stage}", [1, 512], F32) as psB, \
             nc.psum_tensor(f"psM{stage}", [128, 512], F32) as psM, \
             nc.psum_tensor(f"psR{stage}", [128, 512], F32) as psR, \
             nc.psum_tensor(f"pp{stage}", [128, 1024], F32) as pp:
            for tb in range(4):
                p.begin()
                p.wait("sp", "dv")
                p.wait("sp", "pe")
                p.wait("sp", "st")
                p.on("sp", lambda e, tb=tb: e.dma_start(
                    out=xch[:],
                    in_=src.rearrange("(c p) n -> p c n", p=128)[:, :, 512 * tb:512 * (tb + 1)]),
                    "ld", 16)
                p.flush()
                emit_ln_block(xch, lnbf, 512, psA, psB, psM, psR, also_f32=(stage == 1))
                if stage == 1:
                    p.begin()
                    p.wait("pe", "dv")
                    for dc in range(8):
                        p.on("pe", lambda t, dc=dc: t.matmul(
                            pp[0:8, 0:512], wrt_sb[:, 8 * dc:8 * (dc + 1)],
                            xch[:, 512 * dc:512 * (dc + 1)],
                            start=(dc == 0), stop=(dc == 7)), "pe" if dc == 7 else None)
                    p.wait("dve", "pe")
                    p.wait("dve", "st")
                    p.on("dve", lambda v: v.tensor_copy(srow_sb[:], pp[0:8, 0:512]), "dv")
                    p.wait("act", "dv")
                    p.on("act", lambda s, tb=tb: s.dma_start(
                        out=ags_in[:, 512 * tb:512 * (tb + 1)], in_=srow_sb[:]), "st", 16)
                    p.flush()
                wq = w_q_a if stage == 0 else w_q_b
                wk = w_k_a if stage == 0 else w_k_b
                wv = w_v_a if stage == 0 else w_v_b
                qd = qa_d if stage == 0 else qb_d
                kin = agk_in[stage]
                vin = agv_in[stage]
                w_load(wq)
                emit_proj_fm(qd, 0, 8, bq_sb if stage == 0 else None, stage == 1, tb, pp)
                w_load(wk)
                emit_proj_fm(kin, 0, 8, bk_sb if stage == 0 else None, stage == 1, tb, pp)
                w_load(wv)
                emit_proj_v97(vin, stage == 0, tb, pp)

    def emit_ag(stage):
        p.begin()
        p.wait("pool", "st")
        p.on("pool", lambda g, s=stage: g.collective_compute(
            "AllGather", ALU.bypass, replica_groups=RG,
            ins=[agk_in[s].opt()], outs=[agk_out[s].opt()]), "cc")
        p.on("pool", lambda g, s=stage: g.collective_compute(
            "AllGather", ALU.bypass, replica_groups=RG,
            ins=[agv_in[s].opt()], outs=[agv_out[s].opt()]), "cc")
        if stage == 1:
            p.on("pool", lambda g: g.collective_compute(
                "AllGather", ALU.bypass, replica_groups=RG,
                ins=[ags_in.opt()], outs=[ags_out.opt()]), "cc")
        p.wait("pool", "cc")
        p.flush()
        kgl = ka_g if stage == 0 else kb_g
        vgl = va_g if stage == 0 else vb_g
        p.begin()
        for rr in range(4):
            for j in range(NQMT):
                dcol = PAD + 1024 * j + 256 * _sigma(rr, j)
                p.on("sp", lambda e, rr=rr, j=j, dcol=dcol, s=stage, k=kgl: e.dma_start(
                    out=k[:, dcol:dcol + 256],
                    in_=agk_out[s][1024 * rr:1024 * (rr + 1), 256 * j:256 * (j + 1)]), "ld", 16)
                p.on("sp", lambda e, rr=rr, j=j, dcol=dcol, s=stage, v=vgl: e.dma_start(
                    out=v[dcol:dcol + 256, :],
                    in_=agv_out[s][2048 * rr + 256 * j:2048 * rr + 256 * (j + 1), :]), "ld", 16)
                if stage == 1:
                    p.on("sp", lambda e, rr=rr, j=j, dcol=dcol: e.dma_start(
                        out=s_glob[:, dcol - PAD:dcol - PAD + 256],
                        in_=ags_out[8 * rr:8 * (rr + 1), 256 * j:256 * (j + 1)]), "ld", 16)
        p.flush()

    def emit_selection():
        with nc.psum_tensor("pcnt", [8, 1], F32) as pcnt, \
             nc.psum_tensor("ptau", [128, 1], F32) as ptau:
            p.begin()
            p.wait("sp", "ld")
            for h in range(8):
                p.on("sp", lambda e, h=h: e.dma_start(
                    out=bis_S[16 * h:16 * (h + 1), :],
                    in_=s_glob[h:h + 1, :].rearrange("o (s f) -> (o s) f", f=512)), "ld", 16)
            p.wait("dve", "ld")
            p.on("dve", lambda v: v.memset(bis_lo[:], -512.0))
            p.on("dve", lambda v: v.memset(bis_hi[:], 512.0), "dv")
            p.flush()
            p.begin()
            for it in range(BITERS):
                p.on("dve", lambda v: v.drain())
                p.on("dve", lambda v: v.tensor_tensor(bis_mid[:], bis_lo[:], bis_hi[:], ALU.add))
                p.on("dve", lambda v: v.drain())
                p.on("dve", lambda v: v.tensor_scalar(bis_mid[:], bis_mid[:], 0.5, None, ALU.mult))
                p.on("dve", lambda v: v.drain(), "dv")
                p.wait("pe", "dv")
                p.on("pe", lambda t: t.matmul(ptau[:], hselT_sb[:], bis_mid[:], start=True, stop=True), "pe")
                p.wait("dve", "pe")
                p.on("dve", lambda v: v.tensor_scalar(bis_tmp[:], bis_S[:], ptau[:, 0:1], None, ALU.is_ge))
                p.on("dve", lambda v: v.tensor_reduce(bis_red[:], bis_tmp[:], mybir.AxisListType.X, ALU.add))
                p.on("dve", lambda v: v.drain(), "dv")
                p.wait("pe", "dv")
                p.on("pe", lambda t: t.matmul(pcnt[:], hsel128_sb[:], bis_red[:], start=True, stop=True), "pe")
                p.wait("dve", "pe")
                p.on("dve", lambda v: v.tensor_scalar(bis_ge[:], pcnt[:], float(SEL_K), None, ALU.is_ge))
                p.on("dve", lambda v: v.drain())
                p.on("dve", lambda v: v.select(bis_lo[:], bis_ge[:], bis_mid[:], bis_lo[:]))
                p.on("dve", lambda v: v.select(bis_hi[:], bis_ge[:], bis_hi[:], bis_mid[:]), "dv")
            p.flush()
            p.begin()
            p.on("dve", lambda v: v.drain(), "dv")
            p.wait("pe", "dv")
            p.on("pe", lambda t: t.matmul(ptau[:], hselT_sb[:], bis_lo[:], start=True, stop=True), "pe")
            p.wait("dve", "pe")
            p.on("dve", lambda v: v.tensor_scalar(bis_tmp[:], bis_S[:], ptau[:, 0:1], None, ALU.is_ge))
            p.on("dve", lambda v: v.tensor_copy(selbf[:], bis_tmp[:]), "dv")
            p.wait("act", "dv")
            p.on("act", lambda s: s.activation(gts[:], bis_S[:], AF.Exp, scale=-1.0), "ex")
            p.wait("dve", "ex")
            p.on("dve", lambda v: v.tensor_scalar(gts[:], gts[:], 1.0, None, ALU.add), "dv")
            p.wait("act", "dv")
            p.on("act", lambda s: s.activation(gts[:], gts[:], AF.Ln))
            p.on("act", lambda s: s.activation(gts[:], gts[:], AF.Exp, scale=-1.0), "ex")
            p.wait("dve", "ex")
            p.on("dve", lambda v: v.tensor_tensor(gts[:], gts[:], bis_tmp[:], ALU.mult), "dv")
            p.wait("act", "dv")
            p.on("act", lambda s: s.dma_start(
                out=selb_d[0:8, :].rearrange("o (s f) -> (o s) f", f=512), in_=selbf[:]), "st", 16)
            p.on("act", lambda s: s.dma_start(
                out=gates_d.rearrange("o (s f) -> (o s) f", f=512), in_=gts[:]), "st", 16)
            p.on("act", lambda s: s.dma_start(
                out=selb_d[8:16, :].rearrange("o (s f) -> (o s) f", f=512), in_=selbf[:]), "st", 16)
            p.flush()
        if ABL_NOVM:
            return
        p.begin()
        p.wait("sp", "st")
        p.wait("sp", "ld")
        dv_done = [0, 0]
        st_done = [0, 0]
        for tt in range(64):
            b = tt % 2
            p.wait("sp", "dv", dv_done[b])
            p.wait("sp", "st", st_done[b])
            p.wait("sp", "dv")
            p.on("sp", lambda e, tt=tt, b=b: e.dma_start(
                out=vm_sb[b][:], in_=vb_g[PAD + 128 * tt:PAD + 128 * (tt + 1), 768:1536]), "ld", 16)
            p.on("sp", lambda e, tt=tt, b=b: e.dma_start(
                out=selT_sb[:], in_=selb_d[:, 128 * tt:128 * (tt + 1)], transpose=True), "ld", 16)
            p.wait("dve", "ld")
            p.wait("dve", "st", p.cnt("st"))
            p.on("dve", lambda v: v.tensor_copy(selTf_sb[:], selT_sb[:]))
            p.on("dve", lambda v: v.drain())
            for h in range(8):
                p.on("dve", lambda v, h=h, b=b: v.tensor_scalar(
                    vm_sb[b][:, 96 * h:96 * (h + 1)], vm_sb[b][:, 96 * h:96 * (h + 1)],
                    selTf_sb[:, h + 8:h + 9], None, ALU.mult), "dv" if h == 7 else None)
            dv_done[b] = p.cnt("dv")
            p.wait("act", "dv", dv_done[b])
            p.on("act", lambda s, tt=tt, b=b: s.dma_start(
                out=vb_g[PAD + 128 * tt:PAD + 128 * (tt + 1), 768:1536], in_=vm_sb[b][:]), "st", 16)
            st_done[b] = p.cnt("st")
        p.flush()

    def emit_attention(stage):
        kgl = ka_g if stage == 0 else kb_g
        vgl = va_g if stage == 0 else vb_g
        qd = qa_d if stage == 0 else qb_d
        with nc.psum_tensor(f"pscA{stage}", [128, 1024], F32) as pscA, \
             nc.psum_tensor(f"pscB{stage}", [128, 1024], F32) as pscB, \
             nc.psum_tensor(f"pav0{stage}", [96, 256], F32) as pav0, \
             nc.psum_tensor(f"pav1{stage}", [96, 256], F32) as pav1, \
             nc.psum_tensor(f"pav2{stage}", [96, 256], F32) as pav2, \
             nc.psum_tensor(f"pav3{stage}", [96, 256], F32) as pav3:
            psc = [pscA, pscB]
            pav = [pav0, pav1, pav2, pav3]
            pbc = pscB[0:64, 0:256]
            pout = pscA[0:128, 0:256]
            psum_r = pscB[0:1, 512:768]
            psq_r = pscB[0:1, 768:1024]
            pm_b = pscA[0:128, 256:512]
            pr_b = pscA[0:128, 512:768]

            gidx = [0]
            kvseq = [0]
            last_ex_of_buf = {0: 0, 1: 0}
            last_av_of_kv = {0: 0, 1: 0}
            pt_last_av = {0: 0, 1: 0}
            last_q_av = [0]

            p.begin()
            p.wait("sp", "st")
            p.wait("pool", "ld")
            p.wait("pool", "st")
            for j in range(NQMT):
                p.wait("sp", "av", last_q_av[0])
                p.wait("sp", "dv")
                for g4 in range(4):
                    p.on("sp", lambda e, j=j, g4=g4, qd=qd: e.dma_start(
                        out=qg[g4][:].rearrange("p (h n) -> p h n", h=4),
                        in_=qd.rearrange("(g h p) n -> g p h n", g=4, p=64)[g4][:, :, 256 * j:256 * (j + 1)]),
                        "ld", 16)
                xsrc = x_in if stage == 0 else x1_d
                p.on("sp", lambda e, j=j, xsrc=xsrc: e.dma_start(
                    out=xq[:], in_=xsrc.rearrange("(c p) n -> p c n", p=128)[:, :, 256 * j:256 * (j + 1)]),
                    "ld", 16)
                q_ld = p.cnt("ld")
                if stage == 1:
                    p.wait("sp", "dv")
                    def load_gates(g, j=j):
                            for h in range(8):
                                nb = len(nc.inst_names())
                                g.reg_load(sreg, sched_sb[0:1, 72 + j:73 + j])
                                g.reg_add(sreg, sreg, h * T)
                                g.dma_start(out=grow[h][:],
                                            in_=bass.AP(gates_d.tensor, sreg, [[T, 1], [1, 256]])
                                            ).then_inc(p.S["gd"], 16)
                                _free_dyn_regs(nc, nb)
                    p.C["gd"] += 16 * 8
                    p.on("sp", load_gates)
                gates_gd = p.cnt("gd")

                for hg in range(4):
                    p.wait("pe", "dv")
                    for mi, m in enumerate(range(j, -1, -1)):
                        u = j * (j + 1) // 2 + mi
                        b = kvseq[0] % 2
                        kvseq[0] += 1
                        p.wait("sp", "av", last_av_of_kv[b])

                        def load_kv(g, u=u, b=b, hg=hg, kgl=kgl, vgl=vgl):
                                nb = len(nc.inst_names())
                                g.reg_load(sreg, sched_sb[0:1, 2 * u:2 * u + 1])
                                g.reg_add(sreg, sreg, 256 * hg * TP)
                                g.dma_start(
                                    out=kg[b][:].rearrange("p (h n) -> p h n", h=4),
                                    in_=bass.AP(kgl.tensor, sreg, [[TP, 64], [64 * TP, 4], [1, 1024]])
                                ).then_inc(p.S["gd"], 16)
                                g.reg_load(sreg, sched_sb[0:1, 2 * u + 1:2 * u + 2])
                                g.reg_add(sreg, sreg, 384 * hg)
                                g.dma_start(
                                    out=vg[b][:].rearrange("p (s c) -> p s c", s=8),
                                    in_=bass.AP(vgl.tensor, sreg, [[VW, 128], [128 * VW, 8], [1, 384]])
                                ).then_inc(p.S["gd"], 16)
                                _free_dyn_regs(nc, nb)
                        p.C["gd"] += 32
                        p.on("sp", load_kv)
                        kv_gd = p.cnt("gd")

                        for h4 in range(4):
                            grp_info = []
                            for grp in range(2):
                                gb = gidx[0] % 2
                                gidx[0] += 1
                                p.wait("pe", "ex", last_ex_of_buf[gb])
                                p.wait("pe", "gd", kv_gd)
                                p.wait("pe", "ld", q_ld)
                                for kk in range(4):
                                    vv = 4 * grp + kk
                                    p.on("pe", lambda t, gb=gb, b=b, hg=hg, h4=h4, vv=vv, kk=kk: t.matmul(
                                        psc[gb][:, 256 * kk:256 * (kk + 1)],
                                        kg[b][:, 1024 * h4 + 128 * vv:1024 * h4 + 128 * (vv + 1)],
                                        qg[hg][:, 256 * h4:256 * (h4 + 1)],
                                        start=True, stop=True), "pe" if kk == 3 else None)
                                sc_pe = p.cnt("pe")
                                p.wait("act", "pe", sc_pe)
                                p.wait("act", "av", pt_last_av[gb])
                                p.on("act", lambda s, gb=gb: s.activation(
                                    pt[gb][:], psc[gb][:], AF.Exp, bias=kconst[:, 2:3], scale=0.125), "ex")
                                last_ex_of_buf[gb] = p.cnt("ex")
                                ex_here = p.cnt("ex")
                                need_ms = None
                                if m == 0 and grp == 1:
                                    p.wait("pool", "ex", ex_here)
                                    p.on("pool", lambda g, gb=gb: g.affine_select(
                                        pt[gb][:, 512:768], pt[gb][:, 512:768], [[1, 256]],
                                        ALU.is_ge, 0.0, base=0, channel_multiplier=-1))
                                    p.on("pool", lambda g, gb=gb: g.affine_select(
                                        pt[gb][:, 768:1024], pt[gb][:, 768:1024], [[1, 256]],
                                        ALU.is_ge, 0.0, base=-128, channel_multiplier=-1), "ms")
                                    need_ms = p.cnt("ms")
                                grp_info.append((gb, ex_here, need_ms, grp))
                            for gb, ex_here, need_ms, grp in grp_info:
                                if need_ms is not None:
                                    p.wait("pe", "ms", need_ms)
                                else:
                                    p.wait("pe", "ex", ex_here)
                                first = (mi == 0 and grp == 0)
                                last = (m == 0 and grp == 1)
                                for kk in range(4):
                                    vv = 4 * grp + kk
                                    p.on("pe", lambda t, gb=gb, b=b, h4=h4, vv=vv, kk=kk,
                                         first=first, last=last: t.matmul(
                                        pav[h4][0:65, :],
                                        vg[b][:].rearrange("p (s c) -> p s c", s=8)[:, vv, 96 * h4:96 * h4 + 65],
                                        pt[gb][:, 256 * kk:256 * (kk + 1)],
                                        start=(first and kk == 0), stop=(last and kk == 3)),
                                        "av" if kk == 3 else None)
                                pt_last_av[gb] = p.cnt("av")
                        last_av_of_kv[b] = p.cnt("av")

                    for h4 in range(4):
                        hglob = 4 * hg + h4
                        p.wait("act", "av")
                        p.wait("act", "pe")
                        p.wait("act", "dv")
                        p.on("act", lambda s, h4=h4: s.activation(
                            den_sb[:], pav[h4][64:65, :], AF.Ln, bias=kconst[0:1, 1:2]))
                        p.on("act", lambda s: s.activation(
                            den_sb[:], den_sb[:], AF.Exp, scale=-1.0), "ex")
                        if stage == 1 and hg >= 2 and not ABL_NOGATE:
                            p.wait("dve", "ex")
                            p.wait("dve", "gd", gates_gd)
                            p.on("dve", lambda v, hglob=hglob: v.tensor_tensor(
                                den_sb[:], den_sb[:], grow[hglob - 8][:], ALU.mult), "dv")
                            p.wait("pe", "dv")
                        else:
                            p.wait("pe", "ex")
                        p.on("pe", lambda t: t.matmul(
                            pbc, onesf_r[0:1, 0:64], den_sb[:], start=True, stop=True), "pe")
                        p.wait("dve", "pe")
                        p.on("dve", lambda v, h4=h4: v.tensor_copy(av_sb[:], pav[h4][0:64, :]))
                        p.on("dve", lambda v, hglob=hglob: v.tensor_tensor(
                            pair_sb[64 * (hglob % 2):64 * (hglob % 2) + 64,
                                    256 * (hglob // 2):256 * (hglob // 2 + 1)],
                            av_sb[:], pbc, ALU.mult), "dv")
                        p.wait("pe", "dv")

                p.wait("dve", "av")
                for oc in range(8):
                    p.wait("pe", "dv")
                    npc = 4 if (stage == 1 and ABL_NOMOSA) else 8
                    for pc in range(npc):
                        p.on("pe", lambda t, oc=oc, pc=pc, npc=npc: t.matmul(
                            pout, wo_sb[:, 1024 * pc + 128 * oc:1024 * pc + 128 * (oc + 1)],
                            pair_sb[:, 256 * pc:256 * (pc + 1)],
                            start=(pc == 0), stop=(pc == npc - 1)), "pe" if pc == npc - 1 else None)
                    p.wait("dve", "pe")
                    if stage == 0:
                        p.on("dve", lambda v, oc=oc: v.tensor_scalar(
                            aout[:, 256 * oc:256 * (oc + 1)], pout, bo_sb[:, oc:oc + 1], None, ALU.add), "dv")
                    else:
                        p.on("dve", lambda v, oc=oc: v.tensor_copy(
                            aout[:, 256 * oc:256 * (oc + 1)], pout), "dv")
                cast_dv = []
                for i in range(8):
                    p.on("dve", lambda v, i=i: v.tensor_copy(
                        pair_sb[:, 256 * i:256 * (i + 1)], aout[:, 256 * i:256 * (i + 1)]), "dv")
                    cast_dv.append(p.cnt("dv"))
                for i in range(8):
                    p.wait("pe", "dv", cast_dv[i])
                    p.on("pe", lambda t, i=i: t.matmul(
                        psum_r, ones_c[:], pair_sb[:, 256 * i:256 * (i + 1)],
                        start=(i == 0), stop=(i == 7)), "pe" if i == 7 else None)
                sq_dv = []
                sums_pe = p.cnt("pe")
                for i in range(8):
                    if i >= 2:
                        p.wait("dve", "pe", sums_pe)
                    p.on("dve", lambda v, i=i: v.tensor_tensor(
                        pt[i % 2][:, 0:256],
                        pair_sb[:, 256 * i:256 * (i + 1)],
                        pair_sb[:, 256 * i:256 * (i + 1)], ALU.mult), "dv")
                    sq_dv.append(p.cnt("dv"))
                for i in range(8):
                    p.wait("pe", "dv", sq_dv[i])
                    p.on("pe", lambda t, i=i: t.matmul(
                        psq_r, ones_c[:], pt[i % 2][:, 0:256],
                        start=(i == 0), stop=(i == 7)), "pe" if i == 7 else None)
                p.wait("dve", "pe")
                p.on("dve", lambda v: v.tensor_scalar(emrow[:], psum_r, 1.0 / D, None, ALU.mult))
                p.on("dve", lambda v: v.tensor_scalar(evrow[:], psq_r, 1.0 / D, None, ALU.mult))
                p.on("dve", lambda v: v.tensor_tensor(errow[:], emrow[:], emrow[:], ALU.mult))
                p.on("dve", lambda v: v.tensor_tensor(evrow[:], evrow[:], errow[:], ALU.subtract), "dv")
                p.wait("act", "dv")
                p.on("act", lambda s: s.activation(errow[:], evrow[:], AF.Ln, bias=kconst[0:1, 0:1]))
                p.on("act", lambda s: s.activation(errow[:], errow[:], AF.Exp, scale=-0.5), "ex")
                p.wait("pe", "ex")
                p.on("pe", lambda t: t.matmul(pm_b, onesf_r[:], emrow[0:1, :], start=True, stop=True))
                p.on("pe", lambda t: t.matmul(pr_b, onesf_r[:], errow[0:1, :], start=True, stop=True), "pe")
                p.wait("dve", "pe")
                p.wait("dve", "ld", q_ld)
                for i in range(8):
                    sl = slice(256 * i, 256 * (i + 1))
                    p.on("dve", lambda v, sl=sl: v.tensor_tensor(aout[:, sl], aout[:, sl], pm_b, ALU.subtract))
                    p.on("dve", lambda v, sl=sl: v.tensor_tensor(aout[:, sl], aout[:, sl], pr_b, ALU.mult))
                    p.on("dve", lambda v, sl=sl: v.tensor_tensor(xq[:, sl], xq[:, sl], aout[:, sl], ALU.add),
                         "dv" if i == 7 else None)
                p.wait("act", "dv")
                ddst = x1_d if stage == 0 else out_d
                p.on("act", lambda s, j=j, ddst=ddst: s.dma_start(
                    out=ddst.rearrange("(c p) n -> p c n", p=128)[:, :, 256 * j:256 * (j + 1)],
                    in_=xq[:]), "st", 16)
                if stage == 0:
                    p.on("act", lambda s, j=j: s.dma_start(
                        out=dbg_d.rearrange("(c p) n -> p c n", p=128)[:, :, 256 * j:256 * (j + 1)],
                        in_=xq[:]), "st", 16)
                last_q_av[0] = p.cnt("av")
                p.wait("pe", "dv")
            p.flush()

    emit_stage_proj(0)
    emit_ag(0)
    w_load(w_o_a, wo_sb)
    emit_attention(0)
    emit_stage_proj(1)
    emit_ag(1)
    emit_selection()
    w_load(w_o_b, wo_sb)
    emit_attention(1)

    p.begin()
    p.wait("sp", "st")
    p.on("sp", lambda e: e.dma_start(out=dbg_vb[0:128, :], in_=vb_g[PAD:PAD + 128, :]), "ld", 16)
    p.wait("sp", "ld")
    p.flush()

    p.es.close()
    return nc


def _mk_sched(ci):
    sched = np.zeros(128, dtype=np.uint32)
    u = 0
    for j in range(NQMT):
        q0 = 1024 * j + 256 * _sigma(ci, j)
        for mi, m in enumerate(range(j, -1, -1)):
            wincol = PAD + q0 + 256 - 1024 * (m + 1)
            assert 0 <= wincol <= TP - 1024
            sched[2 * u] = wincol
            sched[2 * u + 1] = wincol * VW
            u += 1
        sched[72 + j] = q0
    return sched.reshape(1, 128)


def _rope_tables(ci):
    pos = _own_cols(ci).astype(np.float32)
    inv = (np.float32(10000.0) ** (-(np.arange(16, dtype=np.float32)) / np.float32(16.0)))
    ang = pos[None, :] * inv[:, None]
    cosv = np.cos(ang).astype(np.float32)
    sinv = np.sin(ang).astype(np.float32)
    C = np.zeros((128, 2048), np.float32)
    S = np.zeros((128, 2048), np.float32)
    for blk in range(2):
        r = 64 * blk
        C[r + 0:r + 16] = cosv
        C[r + 16:r + 32] = cosv
        C[r + 32:r + 64] = 1.0
        S[r + 0:r + 16] = -sinv
        S[r + 16:r + 32] = sinv
        S[r + 32:r + 64] = 0.0
    return C, S


def kernel(L0_attn_encodings, L0_attn_attn_W, L0_attn_attn_b, L0_attn_out_W,
           L0_attn_out_b, L0_ffn_mosa_router_w, L0_ffn_mosa_wqkv, L0_ffn_mosa_wo,
           L0_ffn_dense_wqkv, L0_ffn_dense_wo):
    x = np.asarray(L0_attn_encodings)
    attn_W = np.asarray(L0_attn_attn_W)
    attn_b = np.asarray(L0_attn_attn_b)
    out_W = np.asarray(L0_attn_out_W)
    out_b = np.asarray(L0_attn_out_b)
    rt_w = np.asarray(L0_ffn_mosa_router_w)
    m_wqkv = np.asarray(L0_ffn_mosa_wqkv)
    m_wo = np.asarray(L0_ffn_mosa_wo)
    d_wqkv = np.asarray(L0_ffn_dense_wqkv)
    d_wo = np.asarray(L0_ffn_dense_wo)

    qcols = np.concatenate([np.arange(192 * h, 192 * h + 64) for h in range(NH_A)])
    kcols = qcols + 64
    vcols = qcols + 128
    w_q_a = np.ascontiguousarray(attn_W[:, qcols])
    w_k_a = np.ascontiguousarray(attn_W[:, kcols])
    w_v_a = np.ascontiguousarray(attn_W[:, vcols])
    bq = attn_b[qcols]; bk = attn_b[kcols]; bv = attn_b[vcols]
    w_q_b = np.ascontiguousarray(np.concatenate(
        [d_wqkv[:, :, 0:64].reshape(D, 512), m_wqkv[:, :, 0:64].reshape(D, 512)], axis=1))
    w_k_b = np.ascontiguousarray(np.concatenate(
        [d_wqkv[:, :, 64:128].reshape(D, 512), m_wqkv[:, :, 64:128].reshape(D, 512)], axis=1))
    w_v_b = np.ascontiguousarray(np.concatenate(
        [d_wqkv[:, :, 128:192].reshape(D, 512), m_wqkv[:, :, 128:192].reshape(D, 512)], axis=1))
    w_o_b = np.ascontiguousarray(np.concatenate([d_wo, m_wo], axis=0))

    hselT = np.zeros((8, 128), np.float32)
    for h in range(8):
        hselT[h, 16 * h:16 * (h + 1)] = 1.0
    hsel128 = np.ascontiguousarray(hselT.T)

    nc = build_program()
    in_maps = []
    for c in range(NCORE):
        bc, ci = c // 4, c % 4
        cols = _own_cols(ci)
        xc = np.ascontiguousarray(x[bc].T[:, cols])
        C, S = _rope_tables(ci)
        in_maps.append({
            "x": xc,
            "w_q_a": w_q_a, "w_k_a": w_k_a, "w_v_a": w_v_a, "w_o_a": out_W,
            "w_q_b": w_q_b, "w_k_b": w_k_b, "w_v_b": w_v_b, "w_o_b": w_o_b,
            "w_rt": np.ascontiguousarray(rt_w),
            "bq_pp": np.ascontiguousarray(bq.reshape(8, 128).T),
            "bk_pp": np.ascontiguousarray(bk.reshape(8, 128).T),
            "bo_pp": np.ascontiguousarray(out_b.reshape(8, 128).T),
            "bv_row": bv.reshape(1, D),
            "cq_t": C, "sq_t": S,
            "hselT_t": hselT, "hsel128_t": hsel128,
            "sched_t": _mk_sched(ci),
        })
    res = run_bass_kernel_spmd(nc, in_maps, core_ids=list(range(NCORE)))
    out = np.zeros((B, T, D), np.float32)
    for c in range(NCORE):
        bc, ci = c // 4, c % 4
        cols = _own_cols(ci)
        out[bc, cols, :] = np.asarray(res.results[c]["out"]).T
    return out


# version 11
# speedup vs baseline: 1.3421x; 1.0396x over previous
import numpy as np
import ml_dtypes
from contextlib import ExitStack
import concourse.bass as bass
import concourse.mybir as mybir
from concourse.bass_utils import run_bass_kernel_spmd

F32 = mybir.dt.float32
BF16 = mybir.dt.bfloat16
U32 = mybir.dt.uint32
AF = mybir.ActivationFunctionType
ALU = mybir.AluOpType

B, T, D = 2, 8192, 1024
NCORE = 8
NH_A, NH_B, HS = 16, 8, 64
PAD = 768
TP = PAD + T
QMT = 256
NQMT = 8
SHIFT = 20.0
EPS = 1e-5
SEL_K = 1024
BITERS = 46
VW = 16 * 96
SHUF = [(i + 16) % 32 for i in range(32)]
ABL_NOGATE = False
ABL_NOMOSA = False
ABL_NOVM = False
ABL_DUMPDEN = True


def _sigma(ci, j):
    return (ci + j) % 4


def _own_cols(ci):
    idx = []
    for j in range(NQMT):
        base = 1024 * j + 256 * _sigma(ci, j)
        idx.extend(range(base, base + 256))
    return np.asarray(idx, dtype=np.int64)


class PB:
    ENGS = ("sp", "act", "dve", "pool", "pe")

    def __init__(self, nc):
        self.nc = nc
        self.es = ExitStack()
        self.C = {}
        self.S = {}
        for name in ("ld", "st", "gd", "cc", "pe", "ex", "dv", "ms", "av"):
            self.S[name] = self.es.enter_context(nc.semaphore("s_" + name))
            self.C[name] = 0
        self.ops = None

    def sb(self, nm, shp, dt):
        return self.es.enter_context(self.nc.sbuf_tensor(nm, shp, dt))

    def begin(self):
        assert self.ops is None
        self.ops = {e: [] for e in self.ENGS}

    def flush(self):
        ops, self.ops = self.ops, None
        nc = self.nc
        with nc.Block() as blk:
            if ops["sp"]:
                blk.sync(lambda e, _o=ops["sp"]: [f(e) for f in _o])
            if ops["pe"]:
                blk.tensor(lambda e, _o=ops["pe"]: [f(e) for f in _o])
            if ops["act"]:
                blk.scalar(lambda e, _o=ops["act"]: [f(e) for f in _o])
            if ops["dve"]:
                blk.vector(lambda e, _o=ops["dve"]: [f(e) for f in _o])
            if ops["pool"]:
                blk.gpsimd(lambda e, _o=ops["pool"]: [f(e) for f in _o])

    def on(self, eng, fn, sem=None, n=1):
        if sem is not None:
            self.C[sem] += n
            s, k = self.S[sem], n

            def wrapped(e, _fn=fn, _s=s, _k=k):
                ins = _fn(e)
                ins.then_inc(_s, _k)
            self.ops[eng].append(wrapped)
        else:
            self.ops[eng].append(fn)

    def wait(self, eng, sem, val=None):
        v = self.C[sem] if val is None else val
        if v <= 0:
            return
        s = self.S[sem]
        self.ops[eng].append(lambda e, _s=s, _v=v: e.wait_ge(_s, _v))

    def cnt(self, sem):
        return self.C[sem]



_ENG_PREFIX = {"Pool": mybir.EngineType.Pool, "SP": mybir.EngineType.SP,
               "Activation": mybir.EngineType.Activation}


def _free_dyn_regs(nc, nb):
    regs = set()
    for nm in nc.inst_names()[nb:]:
        i2 = nc.find_inst_by_name(nm)
        if i2 is None:
            continue
        ra = i2.regs_accessed() if callable(i2.regs_accessed) else i2.regs_accessed
        for tup in ra:
            for rn in (tup if isinstance(tup, tuple) else (tup,)):
                if '_tmp_' in rn or '_snap_' in rn:
                    regs.add(rn)
    for rn in regs:
        eng = _ENG_PREFIX[rn.split('_', 1)[0]]
        nc.free_register(bass.RegisterHandle(rn, eng))


def build_program():
    nc = bass.Bass("TRN2", target_bir_lowering=False)
    p = PB(nc)

    x_in = nc.dram_tensor("x", [D, 2048], F32, kind="ExternalInput").ap()
    w_q_a = nc.dram_tensor("w_q_a", [D, D], F32, kind="ExternalInput").ap()
    w_k_a = nc.dram_tensor("w_k_a", [D, D], F32, kind="ExternalInput").ap()
    w_v_a = nc.dram_tensor("w_v_a", [D, D], F32, kind="ExternalInput").ap()
    w_o_a = nc.dram_tensor("w_o_a", [D, D], F32, kind="ExternalInput").ap()
    w_q_b = nc.dram_tensor("w_q_b", [D, D], F32, kind="ExternalInput").ap()
    w_k_b = nc.dram_tensor("w_k_b", [D, D], F32, kind="ExternalInput").ap()
    w_v_b = nc.dram_tensor("w_v_b", [D, D], F32, kind="ExternalInput").ap()
    w_o_b = nc.dram_tensor("w_o_b", [D, D], F32, kind="ExternalInput").ap()
    w_rt = nc.dram_tensor("w_rt", [D, 8], F32, kind="ExternalInput").ap()
    bq_pp = nc.dram_tensor("bq_pp", [128, 8], F32, kind="ExternalInput").ap()
    bk_pp = nc.dram_tensor("bk_pp", [128, 8], F32, kind="ExternalInput").ap()
    bo_pp = nc.dram_tensor("bo_pp", [128, 8], F32, kind="ExternalInput").ap()
    bv_row = nc.dram_tensor("bv_row", [1, D], F32, kind="ExternalInput").ap()
    cq_t = nc.dram_tensor("cq_t", [128, 2048], F32, kind="ExternalInput").ap()
    sq_t = nc.dram_tensor("sq_t", [128, 2048], F32, kind="ExternalInput").ap()
    hselT_t = nc.dram_tensor("hselT_t", [8, 128], F32, kind="ExternalInput").ap()
    hsel128_t = nc.dram_tensor("hsel128_t", [128, 8], F32, kind="ExternalInput").ap()
    sched_t = nc.dram_tensor("sched_t", [1, 128], U32, kind="ExternalInput").ap()

    out_d = nc.dram_tensor("out", [D, 2048], F32, kind="ExternalOutput").ap()
    dbg_d = nc.dram_tensor("dbg_x1", [D, 2048], F32, kind="ExternalOutput").ap()
    dbg2_d = nc.dram_tensor("dbg_ffn", [D, 2048], F32, kind="ExternalOutput").ap()
    dbg_den = nc.dram_tensor("dbg_den", [16, 2048], F32, kind="ExternalOutput").ap()
    dbg_vb = nc.dram_tensor("dbg_vb", [256, 1536], BF16, kind="ExternalOutput").ap()

    qa_d = nc.dram_tensor("qa_d", [D, 2048], BF16).ap()
    qb_d = nc.dram_tensor("qb_d", [D, 2048], BF16).ap()
    ka_g = nc.dram_tensor("ka_g", [D, TP], BF16).ap()
    va_g = nc.dram_tensor("va_g", [TP, VW], BF16).ap()
    kb_g = nc.dram_tensor("kb_g", [D, TP], BF16).ap()
    vb_g = nc.dram_tensor("vb_g", [TP, VW], BF16).ap()
    x1_d = nc.dram_tensor("x1_d", [D, 2048], F32).ap()
    s_glob = nc.dram_tensor("s_glob", [8, T], F32).ap()
    gates_d = nc.dram_tensor("gates_d", [8, T], F32).ap()
    selb_d = nc.dram_tensor("selb_d", [16, T], BF16).ap()
    agk_in = [nc.dram_tensor(f"agk_in{s}", [D, 2048], BF16).ap() for s in range(2)]
    agk_out = [nc.dram_tensor(f"agk_out{s}", [4 * D, 2048], BF16).ap() for s in range(2)]
    agv_in = [nc.dram_tensor(f"agv_in{s}", [2048, VW], BF16).ap() for s in range(2)]
    agv_out = [nc.dram_tensor(f"agv_out{s}", [4 * 2048, VW], BF16).ap() for s in range(2)]
    ags_in = nc.dram_tensor("ags_in", [8, 2048], F32).ap()
    ags_out = nc.dram_tensor("ags_out", [32, 2048], F32).ap()

    RG = [[0, 1, 2, 3], [4, 5, 6, 7]]
    preg = p.es.enter_context(nc.gpsimd.register("preg"))
    sreg = p.es.enter_context(nc.sync.register("sreg"))

    sb = p.sb
    w1_sb = sb("w1_sb", [128, 8 * 1024], BF16)
    wo_sb = sb("wo_sb", [128, 8 * 1024], BF16)
    wrt_sb = sb("wrt_sb", [128, 8 * 8], F32)
    bq_sb = sb("bq_sb", [128, 8], F32)
    bk_sb = sb("bk_sb", [128, 8], F32)
    bo_sb = sb("bo_sb", [128, 8], F32)
    bvr_sb = sb("bvr_sb", [128, 1024], BF16)
    cq_sb = sb("cq_sb", [128, 2048], BF16)
    sq_sb = sb("sq_sb", [128, 2048], BF16)
    hselT_sb = sb("hselT_sb", [8, 128], F32)
    hsel128_sb = sb("hsel128_sb", [128, 8], F32)
    sched_sb = sb("sched_sb", [1, 128], U32)
    ones_c = sb("ones_c", [128, 1], BF16)
    kconst = sb("kconst", [128, 3], F32)
    onesf_r = sb("onesf_r", [1, 128], F32)
    zero_big = sb("zero_big", [128, 1536], BF16)
    bv1_sb = sb("bv1_sb", [1, 1024], F32)

    xch = sb("xch", [128, 8 * 512], F32)
    lnbf = sb("lnbf", [128, 8 * 512], BF16)
    sqc = [sb(f"sqc{i}", [128, 512], BF16) for i in range(2)]
    mrow = sb("mrow", [1, 512], F32)
    vrow = sb("vrow", [1, 512], F32)
    rrow = sb("rrow", [1, 512], F32)
    ev = [sb(f"ev{i}", [128, 512], BF16) for i in range(2)]
    ropeC = sb("ropeC", [128, 512], BF16)
    ropeS = sb("ropeS", [128, 512], F32)
    ropeT = sb("ropeT", [128, 512], BF16)
    v97s = sb("v97s", [128, 1536], BF16)
    srow_sb = sb("srow_sb", [8, 512], F32)

    kg = [sb(f"kg{i}", [64, 4096], BF16) for i in range(2)]
    vg = [sb(f"vg{i}", [128, 8 * 384], BF16) for i in range(2)]
    qg = [sb(f"qg{i}", [64, 1024], BF16) for i in range(4)]
    pt = [sb(f"pt{i}", [128, 1024], BF16) for i in range(2)]
    pair_sb = sb("pair_sb", [128, 8 * 256], BF16)
    av_sb = sb("av_sb", [64, 256], BF16)
    den_sb4 = [sb(f"den_sb{i}", [1, 256], F32) for i in range(4)]
    grow = [sb(f"grow{i}", [1, 256], F32) for i in range(8)]
    aout = sb("aout", [128, 8 * 256], F32)
    xq = sb("xq", [128, 8 * 256], F32)
    emrow = sb("emrow", [1, 256], F32)
    evrow = sb("evrow", [1, 256], F32)
    errow = sb("errow", [1, 256], F32)

    bis_S = sb("bis_S", [128, 512], F32)
    bis_tmp = sb("bis_tmp", [128, 512], F32)
    bis_red = sb("bis_red", [128, 1], F32)
    bis_lo = sb("bis_lo", [8, 1], F32)
    bis_hi = sb("bis_hi", [8, 1], F32)
    bis_mid = sb("bis_mid", [8, 1], F32)
    bis_ge = sb("bis_ge", [8, 1], U32)
    selbf = sb("selbf", [128, 512], BF16)
    gts = sb("gts", [128, 512], F32)
    selT_sb = sb("selT_sb", [128, 16], BF16)
    selTf_sb = sb("selTf_sb", [128, 16], F32)
    vm_sb = [sb(f"vm_sb{i}", [128, 768], BF16) for i in range(2)]

    p.begin()
    p.on("pool", lambda g: g.dma_start(out=cq_sb[:], in_=cq_t), "gd", 16)
    p.on("pool", lambda g: g.dma_start(out=sq_sb[:], in_=sq_t), "gd", 16)
    for src, dst in ((w_rt, wrt_sb),):
        p.on("sp", lambda e, s=src, d=dst: e.dma_start(
            out=d[:].rearrange("p (c o) -> p c o", c=8),
            in_=s.rearrange("(c p) o -> p c o", p=128)), "ld", 16)
    for src, dst in ((bq_pp, bq_sb), (bk_pp, bk_sb), (bo_pp, bo_sb),
                     (hselT_t, hselT_sb), (hsel128_t, hsel128_sb),
                     (sched_t, sched_sb), (bv_row, bv1_sb)):
        p.on("sp", lambda e, s=src, d=dst: e.dma_start(out=d[:], in_=s), "ld", 16)
    p.on("dve", lambda v: v.memset(ones_c[:], 1.0), "dv")
    p.on("dve", lambda v: v.memset(kconst[:, 0:1], EPS), "dv")
    p.on("dve", lambda v: v.memset(kconst[:, 1:2], 1e-30), "dv")
    p.on("dve", lambda v: v.memset(kconst[:, 2:3], -SHIFT), "dv")
    p.on("dve", lambda v: v.memset(onesf_r[:], 1.0), "dv")
    p.on("dve", lambda v: v.memset(zero_big[:], 0.0), "dv")
    p.on("dve", lambda v: v.memset(v97s[:], 0.0), "dv")
    for h in range(16):
        p.on("dve", lambda v, h=h: v.memset(v97s[:, 96 * h + 64:96 * h + 65], 1.0), "dv")
    p.flush()

    p.begin()
    for kgl in (ka_g, kb_g):
        for r in range(8):
            p.on("sp", lambda e, k=kgl, r=r: e.dma_start(
                out=k[128 * r:128 * (r + 1), 0:PAD], in_=zero_big[:, 0:PAD]), "ld", 16)
    for vgl in (va_g, vb_g):
        for r in range(6):
            p.on("sp", lambda e, v=vgl, r=r: e.dma_start(
                out=v[128 * r:128 * (r + 1), :], in_=zero_big[:]), "ld", 16)
    p.flush()

    with nc.psum_tensor("pbv", [128, 1024], F32) as pbv:
        p.begin()
        p.wait("pe", "ld")
        p.on("pe", lambda t: t.matmul(pbv[:, 0:512], onesf_r[:], bv1_sb[0:1, 0:512],
                                      start=True, stop=True))
        p.on("pe", lambda t: t.matmul(pbv[:, 512:1024], onesf_r[:], bv1_sb[0:1, 512:1024],
                                      start=True, stop=True), "pe")
        p.wait("dve", "pe")
        p.on("dve", lambda v: v.tensor_copy(bvr_sb[:], pbv[:]), "dv")
        p.flush()

    def w_load(src, dst=None, rearr=True):
        dst = w1_sb if dst is None else dst
        p.begin()
        p.wait("pool", "pe")
        if rearr:
            ap = src.rearrange("(c p) o -> p c o", p=128)
            p.on("pool", lambda g, s=ap, d=dst: g.dma_start(
                out=d[:].rearrange("p (c o) -> p c o", c=8), in_=s), "gd", 16)
        else:
            p.on("pool", lambda g, s=src, d=dst: g.dma_start(out=d[:], in_=s), "gd", 16)
        p.flush()

    def emit_ln_block(src_f32, out_bf, n, psA, psB, psM, psR, also_f32=False):
        p.begin()
        p.wait("dve", "ld")
        p.wait("dve", "gd")
        p.wait("dve", "pe")
        p.wait("dve", "st")
        p.wait("dve", "ex")
        cast_dv = []
        for i in range(8):
            p.on("dve", lambda v, i=i: v.tensor_copy(
                out_bf[:, n * i:n * (i + 1)], src_f32[:, n * i:n * (i + 1)]), "dv")
            cast_dv.append(p.cnt("dv"))
        for i in range(8):
            p.wait("pe", "dv", cast_dv[i])
            p.on("pe", lambda t, i=i: t.matmul(
                psA[0:1, 0:n], ones_c[:], out_bf[:, n * i:n * (i + 1)],
                start=(i == 0), stop=(i == 7)), "pe" if i == 7 else None)
        sq_dv = []
        for i in range(8):
            if i >= 2:
                p.wait("dve", "pe", p.cnt("pe"))
            p.on("dve", lambda v, i=i: v.tensor_tensor(
                sqc[i % 2][:], out_bf[:, n * i:n * (i + 1)],
                out_bf[:, n * i:n * (i + 1)], ALU.mult), "dv")
            sq_dv.append(p.cnt("dv"))
        for i in range(8):
            p.wait("pe", "dv", sq_dv[i])
            p.on("pe", lambda t, i=i: t.matmul(
                psB[0:1, 0:n], ones_c[:], sqc[i % 2][:],
                start=(i == 0), stop=(i == 7)), "pe" if i == 7 else None)
        p.flush()
        p.begin()
        p.wait("dve", "pe")
        p.on("dve", lambda v: v.tensor_scalar(mrow[:, 0:n], psA[0:1, 0:n], 1.0 / D, None, ALU.mult))
        p.on("dve", lambda v: v.tensor_scalar(vrow[:, 0:n], psB[0:1, 0:n], 1.0 / D, None, ALU.mult))
        p.on("dve", lambda v: v.tensor_tensor(rrow[:, 0:n], mrow[:, 0:n], mrow[:, 0:n], ALU.mult))
        p.on("dve", lambda v: v.tensor_tensor(vrow[:, 0:n], vrow[:, 0:n], rrow[:, 0:n], ALU.subtract), "dv")
        p.wait("act", "dv")
        p.on("act", lambda s: s.activation(rrow[:, 0:n], vrow[:, 0:n], AF.Ln, bias=kconst[0:1, 0:1]))
        p.on("act", lambda s: s.activation(rrow[:, 0:n], rrow[:, 0:n], AF.Exp, scale=-0.5), "ex")
        p.wait("pe", "ex")
        p.on("pe", lambda t: t.matmul(psM[:, 0:n], onesf_r[:], mrow[0:1, 0:n], start=True, stop=True))
        p.on("pe", lambda t: t.matmul(psR[:, 0:n], onesf_r[:], rrow[0:1, 0:n], start=True, stop=True), "pe")
        p.wait("dve", "pe")
        for i in range(8):
            sl = slice(n * i, n * (i + 1))
            if also_f32:
                p.on("dve", lambda v, sl=sl: v.tensor_tensor(src_f32[:, sl], src_f32[:, sl], psM[:, 0:n], ALU.subtract))
                p.on("dve", lambda v, sl=sl: v.tensor_tensor(src_f32[:, sl], src_f32[:, sl], psR[:, 0:n], ALU.mult))
                p.on("dve", lambda v, sl=sl: v.tensor_copy(out_bf[:, sl], src_f32[:, sl]), "dv")
            else:
                p.on("dve", lambda v, sl=sl: v.tensor_tensor(out_bf[:, sl], out_bf[:, sl], psM[:, 0:n], ALU.subtract))
                p.on("dve", lambda v, sl=sl: v.tensor_tensor(out_bf[:, sl], out_bf[:, sl], psR[:, 0:n], ALU.mult), "dv")
        p.flush()

    def emit_proj_fm(dest, dest_r0, nout, bias_sb, rope, tb, pp, n=512):
        p.begin()
        for oc in range(nout):
            p.wait("pe", "gd")
            p.wait("pe", "dv")
            for dc in range(8):
                p.on("pe", lambda t, dc=dc, oc=oc: t.matmul(
                    pp[:, 0:n], w1_sb[:, 1024 * dc + 128 * oc:1024 * dc + 128 * (oc + 1)],
                    lnbf[:, n * dc:n * (dc + 1)],
                    start=(dc == 0), stop=(dc == 7)), "pe" if dc == 7 else None)
            p.wait("dve", "pe")
            p.wait("dve", "st")
            if rope:
                csl = slice(n * tb, n * (tb + 1))
                p.on("dve", lambda v, csl=csl: v.tensor_tensor(ropeC[:], pp[:, 0:n], cq_sb[:, csl], ALU.mult))
                p.on("dve", lambda v: v.stream_shuffle(ropeS[:], pp[:, 0:n], mask=SHUF))
                p.on("dve", lambda v, csl=csl: v.tensor_tensor(ropeT[:], ropeS[:], sq_sb[:, csl], ALU.mult))
                p.on("dve", lambda v: v.tensor_tensor(ropeC[:], ropeC[:], ropeT[:], ALU.add), "dv")
                src_t = ropeC
            else:
                if bias_sb is not None:
                    p.on("dve", lambda v, oc=oc: v.tensor_scalar(
                        ev[0][:], pp[:, 0:n], bias_sb[:, oc:oc + 1], None, ALU.add), "dv")
                else:
                    p.on("dve", lambda v: v.tensor_copy(ev[0][:], pp[:, 0:n]), "dv")
                src_t = ev[0]
            p.wait("act", "dv")
            p.on("act", lambda s, oc=oc, src_t=src_t: s.dma_start(
                out=dest[dest_r0 + 128 * oc:dest_r0 + 128 * (oc + 1), n * tb:n * (tb + 1)],
                in_=src_t[:]), "st", 16)
        p.flush()

    def emit_proj_v97(dest, withbias, tb, pp, n=512):
        p.begin()
        for tt in range(n // 128):
            p.wait("pe", "gd")
            p.wait("pe", "dv")
            for dc in range(8):
                for hh in range(2):
                    p.on("pe", lambda t, dc=dc, hh=hh, tt=tt: t.matmul(
                        pp[:, 512 * hh:512 * (hh + 1)],
                        lnbf[:, n * dc + 128 * tt:n * dc + 128 * (tt + 1)],
                        w1_sb[:, 1024 * dc + 512 * hh:1024 * dc + 512 * (hh + 1)],
                        start=(dc == 0), stop=(dc == 7)), "pe" if (dc == 7 and hh == 1) else None)
            p.wait("dve", "pe")
            p.wait("dve", "st")
            view = v97s[:].rearrange("p (h c) -> p h c", c=96)[:, :, 0:64]
            srcv = pp[:, 0:1024].rearrange("p (h c) -> p h c", c=64)
            if withbias:
                bsrc = bvr_sb[:].rearrange("p (h c) -> p h c", c=64)
                p.on("dve", lambda v: v.tensor_tensor(view, srcv, bsrc, ALU.add), "dv")
            else:
                p.on("dve", lambda v: v.tensor_copy(view, srcv), "dv")
            p.wait("act", "dv")
            p.on("act", lambda s, tt=tt: s.dma_start(
                out=dest[n * tb + 128 * tt:n * tb + 128 * (tt + 1), :],
                in_=v97s[:]), "st", 16)
        p.flush()

    def emit_stage_proj(stage):
        src = x_in if stage == 0 else x1_d
        with nc.psum_tensor(f"psA{stage}", [1, 512], F32) as psA, \
             nc.psum_tensor(f"psB{stage}", [1, 512], F32) as psB, \
             nc.psum_tensor(f"psM{stage}", [128, 512], F32) as psM, \
             nc.psum_tensor(f"psR{stage}", [128, 512], F32) as psR, \
             nc.psum_tensor(f"pp{stage}", [128, 1024], F32) as pp:
            for tb in range(4):
                p.begin()
                p.wait("sp", "dv")
                p.wait("sp", "pe")
                p.wait("sp", "st")
                p.on("sp", lambda e, tb=tb: e.dma_start(
                    out=xch[:],
                    in_=src.rearrange("(c p) n -> p c n", p=128)[:, :, 512 * tb:512 * (tb + 1)]),
                    "ld", 16)
                p.flush()
                emit_ln_block(xch, lnbf, 512, psA, psB, psM, psR, also_f32=(stage == 1))
                if stage == 1:
                    p.begin()
                    p.wait("pe", "dv")
                    for dc in range(8):
                        p.on("pe", lambda t, dc=dc: t.matmul(
                            pp[0:8, 0:512], wrt_sb[:, 8 * dc:8 * (dc + 1)],
                            xch[:, 512 * dc:512 * (dc + 1)],
                            start=(dc == 0), stop=(dc == 7)), "pe" if dc == 7 else None)
                    p.wait("dve", "pe")
                    p.wait("dve", "st")
                    p.on("dve", lambda v: v.tensor_copy(srow_sb[:], pp[0:8, 0:512]), "dv")
                    p.wait("act", "dv")
                    p.on("act", lambda s, tb=tb: s.dma_start(
                        out=ags_in[:, 512 * tb:512 * (tb + 1)], in_=srow_sb[:]), "st", 16)
                    p.flush()
                wq = w_q_a if stage == 0 else w_q_b
                wk = w_k_a if stage == 0 else w_k_b
                wv = w_v_a if stage == 0 else w_v_b
                qd = qa_d if stage == 0 else qb_d
                kin = agk_in[stage]
                vin = agv_in[stage]
                w_load(wq)
                emit_proj_fm(qd, 0, 8, bq_sb if stage == 0 else None, stage == 1, tb, pp)
                w_load(wk)
                emit_proj_fm(kin, 0, 8, bk_sb if stage == 0 else None, stage == 1, tb, pp)
                w_load(wv)
                emit_proj_v97(vin, stage == 0, tb, pp)

    def emit_ag(stage):
        p.begin()
        p.wait("pool", "st")
        p.on("pool", lambda g, s=stage: g.collective_compute(
            "AllGather", ALU.bypass, replica_groups=RG,
            ins=[agk_in[s].opt()], outs=[agk_out[s].opt()]), "cc")
        p.on("pool", lambda g, s=stage: g.collective_compute(
            "AllGather", ALU.bypass, replica_groups=RG,
            ins=[agv_in[s].opt()], outs=[agv_out[s].opt()]), "cc")
        if stage == 1:
            p.on("pool", lambda g: g.collective_compute(
                "AllGather", ALU.bypass, replica_groups=RG,
                ins=[ags_in.opt()], outs=[ags_out.opt()]), "cc")
        p.wait("pool", "cc")
        p.flush()
        kgl = ka_g if stage == 0 else kb_g
        vgl = va_g if stage == 0 else vb_g
        p.begin()
        for rr in range(4):
            for j in range(NQMT):
                dcol = PAD + 1024 * j + 256 * _sigma(rr, j)
                p.on("sp", lambda e, rr=rr, j=j, dcol=dcol, s=stage, k=kgl: e.dma_start(
                    out=k[:, dcol:dcol + 256],
                    in_=agk_out[s][1024 * rr:1024 * (rr + 1), 256 * j:256 * (j + 1)]), "ld", 16)
                p.on("act", lambda e, rr=rr, j=j, dcol=dcol, s=stage, v=vgl: e.dma_start(
                    out=v[dcol:dcol + 256, :],
                    in_=agv_out[s][2048 * rr + 256 * j:2048 * rr + 256 * (j + 1), :]), "st", 16)
                if stage == 1:
                    p.on("sp", lambda e, rr=rr, j=j, dcol=dcol: e.dma_start(
                        out=s_glob[:, dcol - PAD:dcol - PAD + 256],
                        in_=ags_out[8 * rr:8 * (rr + 1), 256 * j:256 * (j + 1)]), "ld", 16)
        p.flush()

    def emit_selection():
        with nc.psum_tensor("pcnt", [8, 1], F32) as pcnt, \
             nc.psum_tensor("ptau", [128, 1], F32) as ptau:
            p.begin()
            p.wait("sp", "ld")
            for h in range(8):
                p.on("sp", lambda e, h=h: e.dma_start(
                    out=bis_S[16 * h:16 * (h + 1), :],
                    in_=s_glob[h:h + 1, :].rearrange("o (s f) -> (o s) f", f=512)), "ld", 16)
            p.wait("dve", "ld")
            p.on("dve", lambda v: v.memset(bis_lo[:], -512.0))
            p.on("dve", lambda v: v.memset(bis_hi[:], 512.0), "dv")
            p.flush()
            p.begin()
            for it in range(BITERS):
                p.on("dve", lambda v: v.drain())
                p.on("dve", lambda v: v.tensor_tensor(bis_mid[:], bis_lo[:], bis_hi[:], ALU.add))
                p.on("dve", lambda v: v.drain())
                p.on("dve", lambda v: v.tensor_scalar(bis_mid[:], bis_mid[:], 0.5, None, ALU.mult))
                p.on("dve", lambda v: v.drain(), "dv")
                p.wait("pe", "dv")
                p.on("pe", lambda t: t.matmul(ptau[:], hselT_sb[:], bis_mid[:], start=True, stop=True), "pe")
                p.wait("dve", "pe")
                p.on("dve", lambda v: v.tensor_scalar(bis_tmp[:], bis_S[:], ptau[:, 0:1], None, ALU.is_ge))
                p.on("dve", lambda v: v.tensor_reduce(bis_red[:], bis_tmp[:], mybir.AxisListType.X, ALU.add))
                p.on("dve", lambda v: v.drain(), "dv")
                p.wait("pe", "dv")
                p.on("pe", lambda t: t.matmul(pcnt[:], hsel128_sb[:], bis_red[:], start=True, stop=True), "pe")
                p.wait("dve", "pe")
                p.on("dve", lambda v: v.tensor_scalar(bis_ge[:], pcnt[:], float(SEL_K), None, ALU.is_ge))
                p.on("dve", lambda v: v.drain())
                p.on("dve", lambda v: v.select(bis_lo[:], bis_ge[:], bis_mid[:], bis_lo[:]))
                p.on("dve", lambda v: v.select(bis_hi[:], bis_ge[:], bis_hi[:], bis_mid[:]), "dv")
            p.flush()
            p.begin()
            p.on("dve", lambda v: v.drain(), "dv")
            p.wait("pe", "dv")
            p.on("pe", lambda t: t.matmul(ptau[:], hselT_sb[:], bis_lo[:], start=True, stop=True), "pe")
            p.wait("dve", "pe")
            p.on("dve", lambda v: v.tensor_scalar(bis_tmp[:], bis_S[:], ptau[:, 0:1], None, ALU.is_ge))
            p.on("dve", lambda v: v.tensor_copy(selbf[:], bis_tmp[:]), "dv")
            p.wait("act", "dv")
            p.on("act", lambda s: s.activation(gts[:], bis_S[:], AF.Exp, scale=-1.0), "ex")
            p.wait("dve", "ex")
            p.on("dve", lambda v: v.tensor_scalar(gts[:], gts[:], 1.0, None, ALU.add), "dv")
            p.wait("act", "dv")
            p.on("act", lambda s: s.activation(gts[:], gts[:], AF.Ln))
            p.on("act", lambda s: s.activation(gts[:], gts[:], AF.Exp, scale=-1.0), "ex")
            p.wait("dve", "ex")
            p.on("dve", lambda v: v.tensor_tensor(gts[:], gts[:], bis_tmp[:], ALU.mult), "dv")
            p.wait("act", "dv")
            p.on("act", lambda s: s.dma_start(
                out=selb_d[0:8, :].rearrange("o (s f) -> (o s) f", f=512), in_=selbf[:]), "st", 16)
            p.on("act", lambda s: s.dma_start(
                out=gates_d.rearrange("o (s f) -> (o s) f", f=512), in_=gts[:]), "st", 16)
            p.on("act", lambda s: s.dma_start(
                out=selb_d[8:16, :].rearrange("o (s f) -> (o s) f", f=512), in_=selbf[:]), "st", 16)
            p.flush()
        if ABL_NOVM:
            return
        p.begin()
        p.wait("sp", "st")
        p.wait("sp", "ld")
        dv_done = [0, 0]
        st_done = [0, 0]
        for tt in range(64):
            b = tt % 2
            p.wait("sp", "dv", dv_done[b])
            p.wait("sp", "st", st_done[b])
            p.wait("sp", "dv")
            p.on("sp", lambda e, tt=tt, b=b: e.dma_start(
                out=vm_sb[b][:], in_=vb_g[PAD + 128 * tt:PAD + 128 * (tt + 1), 768:1536]), "ld", 16)
            p.on("sp", lambda e, tt=tt, b=b: e.dma_start(
                out=selT_sb[:], in_=selb_d[:, 128 * tt:128 * (tt + 1)], transpose=True), "ld", 16)
            p.wait("dve", "ld")
            p.wait("dve", "st", p.cnt("st"))
            p.on("dve", lambda v: v.tensor_copy(selTf_sb[:], selT_sb[:]))
            p.on("dve", lambda v: v.drain())
            for h in range(8):
                p.on("dve", lambda v, h=h, b=b: v.tensor_scalar(
                    vm_sb[b][:, 96 * h:96 * (h + 1)], vm_sb[b][:, 96 * h:96 * (h + 1)],
                    selTf_sb[:, h + 8:h + 9], None, ALU.mult), "dv" if h == 7 else None)
            dv_done[b] = p.cnt("dv")
            p.wait("act", "dv", dv_done[b])
            p.on("act", lambda s, tt=tt, b=b: s.dma_start(
                out=vb_g[PAD + 128 * tt:PAD + 128 * (tt + 1), 768:1536], in_=vm_sb[b][:]), "st", 16)
            st_done[b] = p.cnt("st")
        p.flush()

    def emit_attention(stage):
        kgl = ka_g if stage == 0 else kb_g
        vgl = va_g if stage == 0 else vb_g
        qd = qa_d if stage == 0 else qb_d
        with nc.psum_tensor(f"pscA{stage}", [128, 1024], F32) as pscA, \
             nc.psum_tensor(f"pscB{stage}", [128, 1024], F32) as pscB, \
             nc.psum_tensor(f"pav0{stage}", [96, 256], F32) as pav0, \
             nc.psum_tensor(f"pav1{stage}", [96, 256], F32) as pav1, \
             nc.psum_tensor(f"pav2{stage}", [96, 256], F32) as pav2, \
             nc.psum_tensor(f"pav3{stage}", [96, 256], F32) as pav3:
            psc = [pscA, pscB]
            pav = [pav0, pav1, pav2, pav3]
            pout2 = [pscA[0:128, 0:256], pscA[0:128, 512:768]]
            psum_r = pscB[0:1, 512:768]
            psq_r = pscB[0:1, 768:1024]
            pm_b = pscB[0:128, 0:256]
            pr_b = pscB[0:128, 256:512]

            gidx = [0]
            kvseq = [0]
            last_ex_of_buf = {0: 0, 1: 0}
            last_av_of_kv = {0: 0, 1: 0}
            pt_last_av = {0: 0, 1: 0}
            last_q_av = [0]

            p.begin()
            p.wait("sp", "st")
            p.wait("pool", "ld")
            p.wait("pool", "st")
            for j in range(NQMT):
                p.wait("sp", "av", last_q_av[0])
                p.wait("sp", "dv")
                for g4 in range(4):
                    p.on("sp", lambda e, j=j, g4=g4, qd=qd: e.dma_start(
                        out=qg[g4][:].rearrange("p (h n) -> p h n", h=4),
                        in_=qd.rearrange("(g h p) n -> g p h n", g=4, p=64)[g4][:, :, 256 * j:256 * (j + 1)]),
                        "ld", 16)
                xsrc = x_in if stage == 0 else x1_d
                p.on("sp", lambda e, j=j, xsrc=xsrc: e.dma_start(
                    out=xq[:], in_=xsrc.rearrange("(c p) n -> p c n", p=128)[:, :, 256 * j:256 * (j + 1)]),
                    "ld", 16)
                q_ld = p.cnt("ld")
                if stage == 1:
                    p.wait("sp", "dv")
                    def load_gates(g, j=j):
                            for h in range(8):
                                nb = len(nc.inst_names())
                                g.reg_load(sreg, sched_sb[0:1, 72 + j:73 + j])
                                g.reg_add(sreg, sreg, h * T)
                                g.dma_start(out=grow[h][:],
                                            in_=bass.AP(gates_d.tensor, sreg, [[T, 1], [1, 256]])
                                            ).then_inc(p.S["gd"], 16)
                                _free_dyn_regs(nc, nb)
                    p.C["gd"] += 16 * 8
                    p.on("sp", load_gates)
                gates_gd = p.cnt("gd")

                for hg in range(4):
                    p.wait("pe", "dv")
                    for mi, m in enumerate(range(j, -1, -1)):
                        u = j * (j + 1) // 2 + mi
                        b = kvseq[0] % 2
                        kvseq[0] += 1
                        p.wait("sp", "av", last_av_of_kv[b])

                        def load_kv(g, u=u, b=b, hg=hg, kgl=kgl, vgl=vgl):
                                nb = len(nc.inst_names())
                                g.reg_load(sreg, sched_sb[0:1, 2 * u:2 * u + 1])
                                g.reg_add(sreg, sreg, 256 * hg * TP)
                                g.dma_start(
                                    out=kg[b][:].rearrange("p (h n) -> p h n", h=4),
                                    in_=bass.AP(kgl.tensor, sreg, [[TP, 64], [64 * TP, 4], [1, 1024]])
                                ).then_inc(p.S["gd"], 16)
                                g.reg_load(sreg, sched_sb[0:1, 2 * u + 1:2 * u + 2])
                                g.reg_add(sreg, sreg, 384 * hg)
                                g.dma_start(
                                    out=vg[b][:].rearrange("p (s c) -> p s c", s=8),
                                    in_=bass.AP(vgl.tensor, sreg, [[VW, 128], [128 * VW, 8], [1, 384]])
                                ).then_inc(p.S["gd"], 16)
                                _free_dyn_regs(nc, nb)
                        p.C["gd"] += 32
                        p.on("sp", load_kv)
                        kv_gd = p.cnt("gd")

                        for h4 in range(4):
                            grp_info = []
                            for grp in range(2):
                                gb = gidx[0] % 2
                                gidx[0] += 1
                                p.wait("pe", "ex", last_ex_of_buf[gb])
                                p.wait("pe", "gd", kv_gd)
                                p.wait("pe", "ld", q_ld)
                                for kk in range(4):
                                    vv = 4 * grp + kk
                                    p.on("pe", lambda t, gb=gb, b=b, hg=hg, h4=h4, vv=vv, kk=kk: t.matmul(
                                        psc[gb][:, 256 * kk:256 * (kk + 1)],
                                        kg[b][:, 1024 * h4 + 128 * vv:1024 * h4 + 128 * (vv + 1)],
                                        qg[hg][:, 256 * h4:256 * (h4 + 1)],
                                        start=True, stop=True), "pe" if kk == 3 else None)
                                sc_pe = p.cnt("pe")
                                p.wait("act", "pe", sc_pe)
                                p.wait("act", "av", pt_last_av[gb])
                                p.on("act", lambda s, gb=gb: s.activation(
                                    pt[gb][:], psc[gb][:], AF.Exp, bias=kconst[:, 2:3], scale=0.125), "ex")
                                last_ex_of_buf[gb] = p.cnt("ex")
                                ex_here = p.cnt("ex")
                                need_ms = None
                                if m == 0 and grp == 1:
                                    p.wait("pool", "ex", ex_here)
                                    p.on("pool", lambda g, gb=gb: g.affine_select(
                                        pt[gb][:, 512:768], pt[gb][:, 512:768], [[1, 256]],
                                        ALU.is_ge, 0.0, base=0, channel_multiplier=-1))
                                    p.on("pool", lambda g, gb=gb: g.affine_select(
                                        pt[gb][:, 768:1024], pt[gb][:, 768:1024], [[1, 256]],
                                        ALU.is_ge, 0.0, base=-128, channel_multiplier=-1), "ms")
                                    need_ms = p.cnt("ms")
                                grp_info.append((gb, ex_here, need_ms, grp))
                            for gb, ex_here, need_ms, grp in grp_info:
                                if need_ms is not None:
                                    p.wait("pe", "ms", need_ms)
                                else:
                                    p.wait("pe", "ex", ex_here)
                                first = (mi == 0 and grp == 0)
                                last = (m == 0 and grp == 1)
                                for kk in range(4):
                                    vv = 4 * grp + kk
                                    p.on("pe", lambda t, gb=gb, b=b, h4=h4, vv=vv, kk=kk,
                                         first=first, last=last: t.matmul(
                                        pav[h4][0:65, :],
                                        vg[b][:].rearrange("p (s c) -> p s c", s=8)[:, vv, 96 * h4:96 * h4 + 65],
                                        pt[gb][:, 256 * kk:256 * (kk + 1)],
                                        start=(first and kk == 0), stop=(last and kk == 3)),
                                        "av" if kk == 3 else None)
                                pt_last_av[gb] = p.cnt("av")
                        last_av_of_kv[b] = p.cnt("av")

                    p.wait("act", "av")
                    p.wait("act", "pe")
                    p.wait("act", "dv")
                    ex_den = []
                    for h4 in range(4):
                        p.on("act", lambda s, h4=h4: s.activation(
                            den_sb4[h4][:], pav[h4][64:65, :], AF.Ln, bias=kconst[0:1, 1:2]))
                        p.on("act", lambda s, h4=h4: s.activation(
                            den_sb4[h4][:], den_sb4[h4][:], AF.Exp, scale=-1.0), "ex")
                        ex_den.append(p.cnt("ex"))
                    if stage == 1 and hg >= 2 and not ABL_NOGATE:
                        p.wait("dve", "gd", gates_gd)
                        for h4 in range(4):
                            p.wait("dve", "ex", ex_den[h4])
                            p.on("dve", lambda v, h4=h4, hg=hg: v.tensor_tensor(
                                den_sb4[h4][:], den_sb4[h4][:], grow[4 * (hg - 2) + h4][:], ALU.mult),
                                "dv" if h4 == 3 else None)
                        p.wait("pe", "dv")
                    else:
                        p.wait("pe", "ex")
                    pbc4 = [pscB[0:64, 256 * k:256 * (k + 1)] for k in range(4)]
                    for h4 in range(4):
                        p.on("pe", lambda t, h4=h4, pbc4=pbc4: t.matmul(
                            pbc4[h4], onesf_r[0:1, 0:64], den_sb4[h4][:], start=True, stop=True),
                            "pe" if h4 == 3 else None)
                    p.wait("dve", "pe")
                    for h4 in range(4):
                        hglob = 4 * hg + h4
                        p.on("dve", lambda v, h4=h4: v.tensor_copy(av_sb[:], pav[h4][0:64, :]))
                        p.on("dve", lambda v, h4=h4, hglob=hglob, pbc4=pbc4: v.tensor_tensor(
                            pair_sb[64 * (hglob % 2):64 * (hglob % 2) + 64,
                                    256 * (hglob // 2):256 * (hglob // 2 + 1)],
                            av_sb[:], pbc4[h4], ALU.mult), "dv" if h4 == 3 else None)
                    p.wait("pe", "dv")

                p.wait("dve", "av")
                oc_dv = [0, 0]
                for oc in range(8):
                    ob = oc % 2
                    p.wait("pe", "dv", oc_dv[ob])
                    for pc in range(8):
                        p.on("pe", lambda t, oc=oc, pc=pc, ob=ob: t.matmul(
                            pout2[ob], wo_sb[:, 1024 * pc + 128 * oc:1024 * pc + 128 * (oc + 1)],
                            pair_sb[:, 256 * pc:256 * (pc + 1)],
                            start=(pc == 0), stop=(pc == 7)), "pe" if pc == 7 else None)
                    p.wait("dve", "pe")
                    if stage == 0:
                        p.on("dve", lambda v, oc=oc, ob=ob: v.tensor_scalar(
                            aout[:, 256 * oc:256 * (oc + 1)], pout2[ob], bo_sb[:, oc:oc + 1], None, ALU.add), "dv")
                    else:
                        p.on("dve", lambda v, oc=oc, ob=ob: v.tensor_copy(
                            aout[:, 256 * oc:256 * (oc + 1)], pout2[ob]), "dv")
                    oc_dv[ob] = p.cnt("dv")
                cast_dv = []
                for i in range(8):
                    p.on("dve", lambda v, i=i: v.tensor_copy(
                        pair_sb[:, 256 * i:256 * (i + 1)], aout[:, 256 * i:256 * (i + 1)]), "dv")
                    cast_dv.append(p.cnt("dv"))
                for i in range(8):
                    p.wait("pe", "dv", cast_dv[i])
                    p.on("pe", lambda t, i=i: t.matmul(
                        psum_r, ones_c[:], pair_sb[:, 256 * i:256 * (i + 1)],
                        start=(i == 0), stop=(i == 7)), "pe" if i == 7 else None)
                sq_dv = []
                sums_pe = p.cnt("pe")
                for i in range(8):
                    if i >= 2:
                        p.wait("dve", "pe", sums_pe)
                    p.on("dve", lambda v, i=i: v.tensor_tensor(
                        pt[i % 2][:, 0:256],
                        pair_sb[:, 256 * i:256 * (i + 1)],
                        pair_sb[:, 256 * i:256 * (i + 1)], ALU.mult), "dv")
                    sq_dv.append(p.cnt("dv"))
                for i in range(8):
                    p.wait("pe", "dv", sq_dv[i])
                    p.on("pe", lambda t, i=i: t.matmul(
                        psq_r, ones_c[:], pt[i % 2][:, 0:256],
                        start=(i == 0), stop=(i == 7)), "pe" if i == 7 else None)
                p.wait("dve", "pe")
                p.on("dve", lambda v: v.tensor_scalar(emrow[:], psum_r, 1.0 / D, None, ALU.mult))
                p.on("dve", lambda v: v.tensor_scalar(evrow[:], psq_r, 1.0 / D, None, ALU.mult))
                p.on("dve", lambda v: v.tensor_tensor(errow[:], emrow[:], emrow[:], ALU.mult))
                p.on("dve", lambda v: v.tensor_tensor(evrow[:], evrow[:], errow[:], ALU.subtract), "dv")
                p.wait("act", "dv")
                p.on("act", lambda s: s.activation(errow[:], evrow[:], AF.Ln, bias=kconst[0:1, 0:1]))
                p.on("act", lambda s: s.activation(errow[:], errow[:], AF.Exp, scale=-0.5), "ex")
                p.wait("pe", "ex")
                p.on("pe", lambda t: t.matmul(pm_b, onesf_r[:], emrow[0:1, :], start=True, stop=True))
                p.on("pe", lambda t: t.matmul(pr_b, onesf_r[:], errow[0:1, :], start=True, stop=True), "pe")
                p.wait("dve", "pe")
                p.wait("dve", "ld", q_ld)
                for i in range(8):
                    sl = slice(256 * i, 256 * (i + 1))
                    p.on("dve", lambda v, sl=sl: v.tensor_tensor(aout[:, sl], aout[:, sl], pm_b, ALU.subtract))
                    p.on("dve", lambda v, sl=sl: v.tensor_tensor(aout[:, sl], aout[:, sl], pr_b, ALU.mult))
                    p.on("dve", lambda v, sl=sl: v.tensor_tensor(xq[:, sl], xq[:, sl], aout[:, sl], ALU.add),
                         "dv" if i == 7 else None)
                p.wait("act", "dv")
                ddst = x1_d if stage == 0 else out_d
                p.on("act", lambda s, j=j, ddst=ddst: s.dma_start(
                    out=ddst.rearrange("(c p) n -> p c n", p=128)[:, :, 256 * j:256 * (j + 1)],
                    in_=xq[:]), "st", 16)
                if stage == 0:
                    p.on("act", lambda s, j=j: s.dma_start(
                        out=dbg_d.rearrange("(c p) n -> p c n", p=128)[:, :, 256 * j:256 * (j + 1)],
                        in_=xq[:]), "st", 16)
                last_q_av[0] = p.cnt("av")
                p.wait("pe", "dv")
            p.flush()

    emit_stage_proj(0)
    emit_ag(0)
    w_load(w_o_a, wo_sb)
    emit_attention(0)
    emit_stage_proj(1)
    emit_ag(1)
    emit_selection()
    w_load(w_o_b, wo_sb)
    emit_attention(1)

    p.begin()
    p.wait("sp", "st")
    p.on("sp", lambda e: e.dma_start(out=dbg_vb[0:128, :], in_=vb_g[PAD:PAD + 128, :]), "ld", 16)
    p.wait("sp", "ld")
    p.flush()

    p.es.close()
    return nc


def _mk_sched(ci):
    sched = np.zeros(128, dtype=np.uint32)
    u = 0
    for j in range(NQMT):
        q0 = 1024 * j + 256 * _sigma(ci, j)
        for mi, m in enumerate(range(j, -1, -1)):
            wincol = PAD + q0 + 256 - 1024 * (m + 1)
            assert 0 <= wincol <= TP - 1024
            sched[2 * u] = wincol
            sched[2 * u + 1] = wincol * VW
            u += 1
        sched[72 + j] = q0
    return sched.reshape(1, 128)


def _rope_tables(ci):
    pos = _own_cols(ci).astype(np.float32)
    inv = (np.float32(10000.0) ** (-(np.arange(16, dtype=np.float32)) / np.float32(16.0)))
    ang = pos[None, :] * inv[:, None]
    cosv = np.cos(ang).astype(np.float32)
    sinv = np.sin(ang).astype(np.float32)
    C = np.zeros((128, 2048), np.float32)
    S = np.zeros((128, 2048), np.float32)
    for blk in range(2):
        r = 64 * blk
        C[r + 0:r + 16] = cosv
        C[r + 16:r + 32] = cosv
        C[r + 32:r + 64] = 1.0
        S[r + 0:r + 16] = -sinv
        S[r + 16:r + 32] = sinv
        S[r + 32:r + 64] = 0.0
    return C, S


def kernel(L0_attn_encodings, L0_attn_attn_W, L0_attn_attn_b, L0_attn_out_W,
           L0_attn_out_b, L0_ffn_mosa_router_w, L0_ffn_mosa_wqkv, L0_ffn_mosa_wo,
           L0_ffn_dense_wqkv, L0_ffn_dense_wo):
    x = np.asarray(L0_attn_encodings)
    attn_W = np.asarray(L0_attn_attn_W)
    attn_b = np.asarray(L0_attn_attn_b)
    out_W = np.asarray(L0_attn_out_W)
    out_b = np.asarray(L0_attn_out_b)
    rt_w = np.asarray(L0_ffn_mosa_router_w)
    m_wqkv = np.asarray(L0_ffn_mosa_wqkv)
    m_wo = np.asarray(L0_ffn_mosa_wo)
    d_wqkv = np.asarray(L0_ffn_dense_wqkv)
    d_wo = np.asarray(L0_ffn_dense_wo)

    qcols = np.concatenate([np.arange(192 * h, 192 * h + 64) for h in range(NH_A)])
    kcols = qcols + 64
    vcols = qcols + 128
    w_q_a = np.ascontiguousarray(attn_W[:, qcols])
    w_k_a = np.ascontiguousarray(attn_W[:, kcols])
    w_v_a = np.ascontiguousarray(attn_W[:, vcols])
    bq = attn_b[qcols]; bk = attn_b[kcols]; bv = attn_b[vcols]
    w_q_b = np.ascontiguousarray(np.concatenate(
        [d_wqkv[:, :, 0:64].reshape(D, 512), m_wqkv[:, :, 0:64].reshape(D, 512)], axis=1))
    w_k_b = np.ascontiguousarray(np.concatenate(
        [d_wqkv[:, :, 64:128].reshape(D, 512), m_wqkv[:, :, 64:128].reshape(D, 512)], axis=1))
    w_v_b = np.ascontiguousarray(np.concatenate(
        [d_wqkv[:, :, 128:192].reshape(D, 512), m_wqkv[:, :, 128:192].reshape(D, 512)], axis=1))
    w_o_b = np.ascontiguousarray(np.concatenate([d_wo, m_wo], axis=0))

    hselT = np.zeros((8, 128), np.float32)
    for h in range(8):
        hselT[h, 16 * h:16 * (h + 1)] = 1.0
    hsel128 = np.ascontiguousarray(hselT.T)

    nc = build_program()
    in_maps = []
    for c in range(NCORE):
        bc, ci = c // 4, c % 4
        cols = _own_cols(ci)
        xc = np.ascontiguousarray(x[bc].T[:, cols])
        C, S = _rope_tables(ci)
        in_maps.append({
            "x": xc,
            "w_q_a": w_q_a, "w_k_a": w_k_a, "w_v_a": w_v_a, "w_o_a": out_W,
            "w_q_b": w_q_b, "w_k_b": w_k_b, "w_v_b": w_v_b, "w_o_b": w_o_b,
            "w_rt": np.ascontiguousarray(rt_w),
            "bq_pp": np.ascontiguousarray(bq.reshape(8, 128).T),
            "bk_pp": np.ascontiguousarray(bk.reshape(8, 128).T),
            "bo_pp": np.ascontiguousarray(out_b.reshape(8, 128).T),
            "bv_row": bv.reshape(1, D),
            "cq_t": C, "sq_t": S,
            "hselT_t": hselT, "hsel128_t": hsel128,
            "sched_t": _mk_sched(ci),
        })
    res = run_bass_kernel_spmd(nc, in_maps, core_ids=list(range(NCORE)))
    out = np.zeros((B, T, D), np.float32)
    for c in range(NCORE):
        bc, ci = c // 4, c % 4
        cols = _own_cols(ci)
        out[bc, cols, :] = np.asarray(res.results[c]["out"]).T
    return out
